# version 10
# baseline (speedup 1.0000x reference)
import numpy as np
from contextlib import ExitStack

import concourse.bass as bass
import concourse.mybir as mybir
from concourse.bass_utils import run_bass_kernel_spmd

F32 = mybir.dt.float32
BF16 = mybir.dt.bfloat16
F16 = mybir.dt.float16
I32 = mybir.dt.int32
AX = mybir.AxisListType.X
AF = mybir.ActivationFunctionType
ALU = mybir.AluOpType

D, S, H, DK, INNER = 1024, 2048, 16, 64, 4096
NT, NC, NJ = S // 128, D // 128, INNER // 128
EPS = 1e-5
NUM_BUCKETS, MAX_DISTANCE = 32, 128
BW, J0 = 1344, 1445


def _rel_bucket(n):
    nb = NUM_BUCKETS // 2
    ret = (n < 0).astype(np.int64) * nb
    na = np.abs(n)
    mex = nb // 2
    large = mex + (np.log(np.maximum(na, 1) / mex)
                   / np.log(MAX_DISTANCE / mex) * (nb - mex)).astype(np.int64)
    large = np.minimum(large, nb - 1)
    return ret + np.where(na < mex, na, large)


def _near(kc, qb):
    delta = qb * 512 - kc * 128
    if delta >= 218:
        return ("far", 0)
    if delta <= -602:
        return ("far", 1)
    return ("near", delta + 602)


class Prog:
    ENGS = ("sync", "gpsimd", "vector", "scalar", "tensor")

    def __init__(self, sems):
        self.ops = {e: [] for e in self.ENGS}
        self.cnt = {n: 0 for n in sems}
        self.handles = {}

    def dma(self, eng, out, in_, sem, **kw):
        def f(e, out=out, in_=in_, sem=sem, kw=kw):
            e.dma_start(out, in_, **kw).then_inc(self.handles[sem], 16)
        self.ops[eng].append(f)
        self.cnt[sem] += 16
        return self.cnt[sem]

    def dmat(self, eng, out, in_, sem):
        def f(e, out=out, in_=in_, sem=sem):
            e.dma_start_transpose(out, in_).then_inc(self.handles[sem], 16)
        self.ops[eng].append(f)
        self.cnt[sem] += 16
        return self.cnt[sem]

    def wait(self, eng, sem, val):
        if val > 0:
            self.ops[eng].append(lambda e, s=sem, v=val: e.wait_ge(self.handles[s], v))

    def op(self, eng, fn, sem=None):
        def f(e, fn=fn, sem=sem):
            i = fn(e)
            if sem is not None:
                i.then_inc(self.handles[sem], 1)
        self.ops[eng].append(f)
        if sem is not None:
            self.cnt[sem] += 1
            return self.cnt[sem]
        return None

    def opw(self, eng, fn, sem="vv"):
        c = self.op(eng, fn, sem=sem)
        self.wait(eng, sem, c)
        return c


SEMS = ["ldx", "ldw", "ldg", "tp", "vv", "ss", "tt", "gg", "st", "dh",
        "qk", "ex", "av", "dv", "dn",
        "lx0", "lx1", "lm0", "lm1", "lw0", "lw1", "lw2",
        "dh0", "dh1", "fs0", "fs1", "dd", "do", "so0", "so1", "gst"]


class _Stop(Exception):
    pass


def build_program(debug=0, upto=99):
    nc = bass.Bass("TRN2", target_bir_lowering=False, detect_race_conditions=False)
    p = Prog(SEMS)

    x_hbm = nc.dram_tensor("x", [S, D], F32, kind="ExternalInput")
    wqT_h = nc.dram_tensor("wqT", [D, D], F32, kind="ExternalInput")
    wkT_h = nc.dram_tensor("wkT", [D, D], F32, kind="ExternalInput")
    wvT_h = nc.dram_tensor("wvT", [D, D], F32, kind="ExternalInput")
    woT_h = nc.dram_tensor("woT", [D, D], F32, kind="ExternalInput")
    wgT_h = nc.dram_tensor("wgT", [D, INNER], F32, kind="ExternalInput")
    wuT_h = nc.dram_tensor("wuT", [D, INNER], F32, kind="ExternalInput")
    wdT_h = nc.dram_tensor("wdT", [INNER, D], F32, kind="ExternalInput")
    bv2t_h = nc.dram_tensor("bv2t", [H, 128, BW], F32, kind="ExternalInput")
    bfar_h = nc.dram_tensor("bfar", [1, 2 * H], F32, kind="ExternalInput")
    tabT_h = nc.dram_tensor("tabT", [H, NUM_BUCKETS], F32, kind="ExternalInput")
    idm_h = nc.dram_tensor("idm", [128, 128], F32, kind="ExternalInput")
    out_hbm = nc.dram_tensor("out", [S, D], F32, kind="ExternalOutput")

    h_scr = nc.dram_tensor("h_scr", [INNER, S], F32, kind="Internal")
    wdq_scr = nc.dram_tensor("wdq_scr", [INNER, D], BF16, kind="Internal")
    ffn_scr = nc.dram_tensor("ffn_scr", [D, S], F32, kind="Internal")
    att_scr = nc.dram_tensor("att_scr", [D, S], F32, kind="Internal")
    den_scr = nc.dram_tensor("den_scr", [1, 2 * H * 1024], F32, kind="Internal")
    sxr_scr = nc.dram_tensor("sxr_scr", [128, NT], F32, kind="Internal")
    nrm_scr = nc.dram_tensor("nrm_scr", [2, 3, 8], F32, kind="Internal")
    tab_scr = nc.dram_tensor("tab_scr", [16, 1], F32, kind="Internal")
    rcp_scr = nc.dram_tensor("rcp_scr", [1, 2 * H * 1024], F32, kind="Internal")

    dbg = {}
    if debug:
        def dt_(name, shape):
            dbg[name] = nc.dram_tensor("d_" + name, shape, F32, kind="ExternalOutput")
        dt_("xnT", [128, NC, S]); dt_("xqT", [128, NC, S]); dt_("rsxb", [128, S])
        dbg["xntm"] = nc.dram_tensor("d_xntm", [128, NT, D], BF16, kind="ExternalOutput")
        dt_("lnr", [128, 8]); dt_("xt0", [128, D])
        dbg["e0"] = nc.dram_tensor("d_e0", [128, D], BF16, kind="ExternalOutput")
        dbg["e1"] = nc.dram_tensor("d_e1", [128, D], BF16, kind="ExternalOutput")
        if debug >= 2:
            dt_("qT", [128, NC, S]); dt_("kT", [128, NC, S])
            dt_("vt", [128, NT, H, 66]); dt_("cbc", [128, 3 * H]); dt_("mcol", [128, 8])
        if debug >= 3:
            dt_("hscr", [INNER, S]); dt_("sbc", [128, S]); dt_("ebc", [128, S])
            dt_("G0", [128, S]); dt_("U0", [128, S]); dt_("xq2", [128, NC, S]); dt_("wqg0", [128, NC, 128])
        if debug >= 4:
            dt_("ffn", [D, S])
        if debug >= 5:
            dt_("attnT", [128, NC, S]); dt_("att", [D, S])

    esP, es0, es1, es2, es3, es4, es5 = (ExitStack() for _ in range(7))

    def sb(es_, name, shape, dt, **kw):
        return es_.enter_context(nc.sbuf_tensor(name, shape, dt, **kw))

    idmb = sb(esP, "idmb", [128, 128], BF16)
    idmf = sb(esP, "idmf", [128, 128], F32)
    cbc = sb(esP, "cbc", [128, 3 * H], F32)
    sxr_tm = sb(esP, "sxr_tm", [128, NT], F32)
    mcol = sb(esP, "mcol", [128, 8], F32)
    lnrow = sb(esP, "lnrow", [128, 8], F32)
    onesr = sb(esP, "onesr", [1, 128], F32)

    xqT = sb(es2, "xqT", [128, NT, NC, 128], BF16, side="right", is_transpose_dest=True)
    xnT = sb(es1, "xnT", [128, NT, NC, 128], BF16, side="right", is_transpose_dest=True)
    xn_tm = sb(es0, "xn_tm", [128, NT, D], BF16)
    xq_tm = sb(es0, "xq_tm", [128, NT, D], BF16)
    xnf = [sb(es0, f"xnf{i}", [128, D], F32) for i in range(2)]
    yqb = sb(es0, "yqb", [128, D], F32)
    i32q = sb(es0, "i32q", [128, D], I32)
    rbb = sb(es0, "rbb", [128, D], BF16)
    cq1 = sb(es0, "cq1", [128, D], BF16)
    xtile = [sb(es0, f"xtile{i}", [128, D], F32) for i in range(2)]

    x_tiles = x_hbm.rearrange("(n p) d -> n p d", p=128)
    p.dma("gpsimd", idmb[:], idm_h[:], "ldg")
    p.dma("sync", idmf[:], idm_h[:], "ldw")

    vdone = []
    p.op("vector", lambda v: v.memset(onesr[:], 1.0))
    ldx = [p.dma("sync", xtile[0][:], x_tiles[0], "lx0"),
           p.dma("sync", xtile[1][:], x_tiles[1], "lx1")]
    for i in range(NT):
        xt = xtile[i % 2][:]
        xn32 = xnf[i % 2][:]
        lxs = "lx0" if i % 2 == 0 else "lx1"
        p.wait("vector", lxs, ldx[i])
        p.wait("scalar", lxs, ldx[i])
        if i >= 2:
            p.wait("scalar", "vv", vdone[i - 2])
        p.op("vector", lambda v, xt=xt: v.reduce_sum(lnrow[:, 0:1], xt, AX))
        sv = p.op("scalar", lambda sc, xt=xt, xn32=xn32: sc.activation(xn32, xt, AF.Square), sem="ss")
        p.wait("vector", "ss", sv)
        p.opw("vector", lambda v, xn32=xn32: v.reduce_sum(lnrow[:, 1:2], xn32, AX))
        p.opw("vector", lambda v: v.tensor_scalar(lnrow[:, 2:3], lnrow[:, 0:1], 1.0 / D, None, ALU.mult))
        p.opw("vector", lambda v: v.tensor_tensor(lnrow[:, 3:4], lnrow[:, 2:3], lnrow[:, 2:3], ALU.mult))
        p.opw("vector", lambda v: v.tensor_scalar(lnrow[:, 4:5], lnrow[:, 1:2], 1.0 / D, None, ALU.mult))
        p.opw("vector", lambda v: v.tensor_tensor(lnrow[:, 4:5], lnrow[:, 4:5], lnrow[:, 3:4], ALU.subtract))
        vv1 = p.op("vector", lambda v: v.tensor_scalar(lnrow[:, 4:5], lnrow[:, 4:5], EPS, None, ALU.add), sem="vv")
        p.wait("scalar", "vv", vv1)
        sv = p.op("scalar", lambda sc: sc.activation(lnrow[:, 5:6], lnrow[:, 4:5], AF.Sqrt), sem="ss")
        p.wait("vector", "ss", sv)
        p.opw("vector", lambda v: v.reciprocal(lnrow[:, 6:7], lnrow[:, 5:6]))
        p.opw("vector", lambda v: v.tensor_tensor(lnrow[:, 7:8], lnrow[:, 6:7], lnrow[:, 6:7], ALU.mult))
        p.opw("vector", lambda v: v.tensor_tensor(lnrow[:, 7:8], lnrow[:, 7:8], lnrow[:, 4:5], ALU.mult))
        p.opw("vector", lambda v: v.tensor_scalar(lnrow[:, 7:8], lnrow[:, 7:8], -0.5, 1.5, ALU.mult, ALU.add))
        p.opw("vector", lambda v: v.tensor_tensor(lnrow[:, 6:7], lnrow[:, 6:7], lnrow[:, 7:8], ALU.mult))
        p.opw("vector", lambda v: v.tensor_tensor(lnrow[:, 3:4], lnrow[:, 2:3], lnrow[:, 6:7], ALU.mult))
        p.opw("vector", lambda v: v.tensor_scalar(lnrow[:, 3:4], lnrow[:, 3:4], -1.0, None, ALU.mult))
        p.op("vector", lambda v, xt=xt, xn32=xn32: v.tensor_scalar(
            xn32, xt, lnrow[:, 6:7], lnrow[:, 3:4], ALU.mult, ALU.add))
        cpd = p.op("vector", lambda v, i=i, xn32=xn32: v.tensor_copy(xn_tm[:, i, :], xn32),
                   sem="vv" if (debug and i == 0) else None)
        if debug and i == 0:
            p.wait("sync", "vv", cpd)
            p.dma("sync", dbg["xt0"][:], xtile[0][:], "st")
            st0 = p.dma("sync", dbg["e0"][:], xn_tm[:, 0, :], "st")
            p.wait("sync", "st", st0)
        p.opw("vector", lambda v, xn32=xn32: v.reduce_max(lnrow[:, 0:1], xn32, AX, apply_absolute_value=True))
        p.opw("vector", lambda v: v.tensor_scalar(lnrow[:, 0:1], lnrow[:, 0:1], 1e-5, None, ALU.max))
        p.op("vector", lambda v, i=i: v.tensor_scalar(sxr_tm[:, i:i + 1], lnrow[:, 0:1], 1.0 / 127.0, None, ALU.mult))
        p.opw("vector", lambda v: v.reciprocal(lnrow[:, 1:2], lnrow[:, 0:1]))
        p.opw("vector", lambda v: v.tensor_scalar(lnrow[:, 1:2], lnrow[:, 1:2], 127.0, None, ALU.mult))
        yv_ = p.op("vector", lambda v, xn32=xn32: v.tensor_scalar(
            yqb[:], xn32, lnrow[:, 1:2], None, ALU.mult), sem="vv")
        p.wait("scalar", "vv", yv_)
        p.op("scalar", lambda sc: sc.activation(i32q[:], yqb[:], AF.Copy, bias=-1.0, scale=1.0))
        sq2 = p.op("scalar", lambda sc: sc.activation(rbb[:], i32q[:], AF.Copy, bias=0.0, scale=1.0), sem="ss")
        p.wait("vector", "ss", sq2)
        p.op("vector", lambda v: v.tensor_tensor(yqb[:], yqb[:], rbb[:], ALU.subtract))
        p.op("vector", lambda v: v.scalar_tensor_tensor(
            cq1[:], yqb[:], 0.5, rbb[:], ALU.is_ge, ALU.add))
        vdone.append(p.op("vector", lambda v, i=i: v.scalar_tensor_tensor(
            xq_tm[:, i, :], yqb[:], 1.5, cq1[:], ALU.is_ge, ALU.add), sem="vv"))
        if i + 2 < NT:
            p.wait("sync", "vv", vdone[i])
            ldx.append(p.dma("sync", xtile[i % 2][:], x_tiles[i + 2], lxs))

    p.wait("sync", "vv", p.cnt["vv"])
    p.wait("sync", "ldg", p.cnt["ldg"])
    p.wait("sync", "lx0", p.cnt["lx0"])
    p.wait("sync", "lx1", p.cnt["lx1"])
    p.wait("sync", "ldw", p.cnt["ldw"])
    for src_tm, dstT in [(xn_tm, xnT), (xq_tm, xqT)]:
        for tt_ in range(NT):
            tpd = p.dmat("sync", dstT[:, tt_, :, :], src_tm[:, tt_, :], "tp")
            p.wait("sync", "tp", tpd)
    TP_DONE = p.cnt["tp"]
    P0_VV, P0_SS = p.cnt["vv"], p.cnt["ss"]
    p.wait("vector", "tp", TP_DONE)
    p.wait("vector", "ss", P0_SS)
    p.wait("scalar", "tp", TP_DONE)
    p.wait("scalar", "vv", P0_VV)
    p.wait("gpsimd", "tp", TP_DONE)

    if debug:
        stage0 = sb(es0, "dstage0", [128, NC * S // 4], F32)
        p.wait("vector", "tp", TP_DONE)
        for nm, src in [("xnT", xnT), ("xqT", xqT)]:
            for hf in range(4):
                half = NC * S // 4
                vs = p.op("vector", lambda v, src=src, hf=hf, half=half: v.tensor_copy(
                    stage0[:], src[:].rearrange("p n c t -> p (n c t)")[:, hf * half:(hf + 1) * half]), sem="vv")
                p.wait("sync", "vv", vs)
                sd = p.dma("sync", dbg[nm][:].rearrange("p c s -> p (c s)")[:, hf * half:(hf + 1) * half],
                           stage0[:], "st")
                p.wait("vector", "st", sd)
    es0.close()
    qT = sb(esP, "qT", [128, NC, S], BF16)
    kT = sb(esP, "kT", [128, NC, S], BF16)
    vt = sb(esP, "vt", [128, NT, H, 66], BF16)
    if upto < 1:
        return _finish(nc, p, [es5, esP])

    wbuf = sb(es1, "wbuf", [128, NC, 512], BF16)
    wmbuf = [sb(es1, f"wmbuf{i}", [128, 1024], F32) for i in range(2)]
    wsum = sb(es1, "wsum", [128, 32], F32)
    gred = sb(es1, "gred", [128, 1], F32)
    sqb = sb(es1, "sqb", [128, S], BF16)
    hsel = sb(es1, "hsel", [128, 2], BF16)
    normq = sb(es1, "normq", [2, NC], F32)
    normk = sb(es1, "normk", [2, NC], F32)
    tabs = sb(es1, "tabs", [16, NUM_BUCKETS], F32)
    tabm = sb(es1, "tabm", [16, 1], F32)
    bfars = sb(es1, "bfars", [1, 2 * H], F32)
    rowb = sb(es1, "rowb", [1, 128], F32)
    crow = sb(es1, "crow", [1, 3 * H], F32)

    pq = [es1.enter_context(nc.psum_tensor(f"pq{i}", [128, 512], F32)) for i in range(2)]
    p.wait("gpsimd", "tp", TP_DONE)
    ev = []
    vms = p.op("vector", lambda v: v.memset(vt[:].rearrange("p n h w -> p (n h w)"), 1.0), sem="vv")
    p.wait("scalar", "vv", vms)
    p.wait("tensor", "tp", TP_DONE)
    halves = []
    for w_h in (wqT_h, wkT_h, wvT_h):
        for hl in range(2):
            halves.append((w_h, hl))
    for hi, (w_h, hl) in enumerate(halves):
        p.wait("gpsimd", "tt", p.cnt["tt"])
        lw_ = p.dma("gpsimd", wbuf[:],
                    w_h[:, hl * 512:(hl + 1) * 512].rearrange("(c p) d -> p c d", p=128), "ldg")
        p.wait("tensor", "ldg", lw_)
        if hi < 4:
            dstT = qT if hi < 2 else kT
            scl = 0.125 if hi < 2 else 1.0
            for oc4 in range(4):
                oc = hl * 4 + oc4
                for qs in range(4):
                    pqt = pq[(oc4 * 4 + qs) % 2]
                    if len(ev) >= 2:
                        p.wait("tensor", "ss", ev[-2])
                    for ic in range(NC):
                        p.op("tensor", lambda t, ic=ic, oc=oc, oc4=oc4, qs=qs, pqt=pqt: t.matmul(
                            pqt[:], wbuf[:, ic, oc4 * 128:(oc4 + 1) * 128],
                            xnT[:, qs * 4:(qs + 1) * 4, ic, :],
                            start=(ic == 0), stop=(ic == NC - 1)),
                            sem="tt" if ic == NC - 1 else None)
                    tv = p.cnt["tt"]
                    p.wait("scalar", "tt", tv)
                    ev.append(p.op("scalar", lambda sc, dstT=dstT, oc=oc, qs=qs, pqt=pqt, scl=scl: sc.activation(
                        dstT[:, oc, qs * 512:(qs + 1) * 512], pqt[:], AF.Copy, scale=scl), sem="ss"))
        else:
            os_ = hl
            for tt_ in range(NT):
                pvt = pq[tt_ % 2]
                if len(ev) >= 2:
                    p.wait("tensor", "ss", ev[-2])
                for ic in range(NC):
                    p.op("tensor", lambda t, ic=ic, tt_=tt_, pvt=pvt: t.matmul(
                        pvt[:], xnT[:, tt_, ic, :],
                        wbuf[:, ic, :],
                        start=(ic == 0), stop=(ic == NC - 1)),
                        sem="tt" if ic == NC - 1 else None)
                tv = p.cnt["tt"]
                p.wait("scalar", "tt", tv)
                vo = vt[:, tt_, os_ * 8:(os_ + 1) * 8, 0:64]
                vi = pvt[:].rearrange("q (h e) -> q h e", e=64)
                ev.append(p.op("scalar", lambda sc, vo=vo, vi=vi: sc.activation(vo, vi, AF.Copy), sem="ss"))
    QKV_SS = p.cnt["ss"]

    wsums = sb(es1, "wsums", [128, 96], F32)
    onescol = sb(es1, "onescol", [128, 1], F32)
    p.op("vector", lambda v: v.memset(onescol[:], 1.0))
    mean_loads = []
    for wh, nrow, ncol, col in [(wgT_h, D, INNER, 0), (wuT_h, D, INNER, 1), (wdT_h, INNER, D, 2)]:
        for r in range(nrow // 128):
            for cc in range(0, ncol, 1024):
                mean_loads.append((wh, r, cc, min(1024, ncol - cc), col))
    gdone = []
    for i, (wh, r, cc, w_, col) in enumerate(mean_loads):
        bi = i % 2
        if i >= 2:
            p.wait("sync", "vv", gdone[i - 2])
        lms = "lm0" if bi == 0 else "lm1"
        lt = p.dma("sync", wmbuf[bi][:, 0:w_], wh[r * 128:(r + 1) * 128, cc:cc + w_], lms)
        p.wait("vector", lms, lt)
        p.opw("vector", lambda v, bi=bi, w_=w_: v.reduce_sum(
            wsum[:, 0:32], wmbuf[bi][:, 0:w_].rearrange("p (a b) -> p a b", b=32), AX,
            apply_absolute_value=True))
        gdone.append(p.op("vector", lambda v, i=i: v.reduce_sum(
            wsums[:, i:i + 1], wsum[:, 0:32], AX), sem="vv"))
    p.wait("vector", "vv", gdone[-1])
    colsum = wsum[:, 0:3]
    for mi in range(3):
        p.op("vector", lambda v, mi=mi: v.reduce_sum(
            colsum[:, mi:mi + 1], wsums[:, mi * 32:(mi + 1) * 32], AX),
            sem="vv" if mi == 2 else None)
    CS_VV = p.cnt["vv"]
    pnb = es1.enter_context(nc.psum_tensor("pnb", [128, 8], F32))
    pn0 = pnb
    p.wait("tensor", "vv", CS_VV)
    for mi in range(3):
        p.op("tensor", lambda t, mi=mi: t.matmul(
            pn0[0:1, mi:mi + 1], onescol[:], colsum[:, mi:mi + 1], start=True, stop=True),
            sem="tt" if mi == 2 else None)
    MS_TT = p.cnt["tt"]
    p.wait("vector", "tt", MS_TT)
    mstg = rowb[0:1, 96:104]
    p.op("vector", lambda v: v.memset(mstg, 0.0))
    p.op("vector", lambda v: v.tensor_scalar(mstg[:, 3:4], pn0[0:1, 0:1], 1.0 / (D * INNER), None, ALU.mult))
    p.op("vector", lambda v: v.tensor_scalar(mstg[:, 4:5], pn0[0:1, 1:2], 1.0 / (D * INNER), None, ALU.mult))
    p.opw("vector", lambda v: v.tensor_scalar(mstg[:, 6:7], pn0[0:1, 2:3], 1.0 / (D * INNER), None, ALU.mult))
    for c_ in (3, 4, 6):
        fn = p.opw if c_ == 6 else p.op
        fn("vector", lambda v, c_=c_: v.tensor_scalar(mstg[:, c_:c_ + 1], mstg[:, c_:c_ + 1], 1e-5, None, ALU.max))
    p.op("vector", lambda v: v.reciprocal(mstg[:, 0:1], mstg[:, 3:4]))
    p.op("vector", lambda v: v.reciprocal(mstg[:, 1:2], mstg[:, 4:5]))
    p.op("vector", lambda v: v.reciprocal(mstg[:, 2:3], mstg[:, 6:7]))
    p.opw("vector", lambda v: v.tensor_tensor(mstg[:, 5:6], mstg[:, 6:7], mstg[:, 4:5], ALU.mult))
    p.op("vector", lambda v: v.tensor_scalar(mstg[:, 5:6], mstg[:, 5:6], 0.25 / 127.0, None, ALU.mult))
    p.op("vector", lambda v: v.tensor_scalar(mstg[:, 3:4], mstg[:, 3:4], 0.5, None, ALU.mult))
    MCV = p.op("vector", lambda v: v.tensor_scalar(mstg[:, 7:8], mstg[:, 7:8], 0.0, None, ALU.mult), sem="vv")
    p.wait("tensor", "vv", MCV)
    mmb = p.op("tensor", lambda t: t.matmul(pnb[:], onesr[:], mstg, start=True, stop=True), sem="tt")
    p.wait("vector", "tt", mmb)
    MC = p.op("vector", lambda v: v.tensor_copy(mcol[:], pnb[:]), sem="vv")

    pn = es1.enter_context(nc.psum_tensor("pn", [2, 2048], F32))
    p.op("vector", lambda v: v.memset(hsel[:, :], 0.0))
    p.op("vector", lambda v: v.memset(hsel[0:64, 0:1], 1.0))
    vsel = p.op("vector", lambda v: v.memset(hsel[64:128, 1:2], 1.0), sem="vv")
    p.wait("tensor", "vv", vsel)
    p.wait("tensor", "ss", QKV_SS)
    p.wait("vector", "ss", QKV_SS)
    for srcT, dstn in [(qT, normq), (kT, normk)]:
        for c in range(NC):
            vq = p.op("vector", lambda v, srcT=srcT, c=c: v.tensor_tensor(
                sqb[:], srcT[:, c, :], srcT[:, c, :], ALU.mult), sem="vv")
            p.wait("tensor", "vv", vq)
            for qs in range(4):
                p.op("tensor", lambda t, qs=qs: t.matmul(
                    pn[:, qs * 512:(qs + 1) * 512], hsel[:], sqb[:, qs * 512:(qs + 1) * 512],
                    start=True, stop=True), sem="tt" if qs == 3 else None)
            tv = p.cnt["tt"]
            p.wait("vector", "tt", tv)
            p.op("vector", lambda v, dstn=dstn, c=c: v.reduce_max(dstn[:, c:c + 1], pn[:], AX), sem="vv")
    NORM_VV = p.cnt["vv"]
    lt = p.dma("sync", tabs[:], tabT_h[:], "ldw")
    p.wait("vector", "ldw", lt)
    vm = p.op("vector", lambda v: v.reduce_max(tabm[:], tabs[:], AX), sem="vv")
    p.wait("sync", "vv", NORM_VV)
    p.wait("sync", "vv", vm)
    p.dma("sync", nrm_scr[0:2, 0, :], normq[:], "tp")
    p.dma("sync", nrm_scr[0:2, 1, :], normk[:], "tp")
    p.dma("sync", tab_scr[:], tabm[:], "tp")
    p.wait("sync", "tp", p.cnt["tp"])
    p.dma("sync", rowb[0:1, 0:16], nrm_scr[:, 0, :].rearrange("q c -> c q").unsqueeze(0), "tp",
          allow_slow_non_contiguous=True)
    p.dma("sync", rowb[0:1, 16:32], nrm_scr[:, 1, :].rearrange("q c -> c q").unsqueeze(0), "tp",
          allow_slow_non_contiguous=True)
    p.dma("sync", rowb[0:1, 32:48], tab_scr[:].rearrange("h o -> o h"), "tp",
          allow_slow_non_contiguous=True)
    lb = p.dma("sync", bfars[:], bfar_h[:], "ldw")
    p.wait("scalar", "tp", p.cnt["tp"])
    sc1 = p.op("scalar", lambda s_: s_.activation(rowb[0:1, 48:64], rowb[0:1, 0:16], AF.Sqrt), sem="ss")
    sc2 = p.op("scalar", lambda s_: s_.activation(rowb[0:1, 64:80], rowb[0:1, 16:32], AF.Sqrt), sem="ss")
    p.wait("vector", "ss", sc2)
    p.wait("vector", "ldw", lb)
    p.opw("vector", lambda v: v.tensor_tensor(rowb[0:1, 48:64], rowb[0:1, 48:64], rowb[0:1, 64:80], ALU.mult))
    p.opw("vector", lambda v: v.tensor_tensor(rowb[0:1, 48:64], rowb[0:1, 48:64], rowb[0:1, 32:48], ALU.add))
    p.op("vector", lambda v: v.tensor_tensor(crow[0:1, 0:16], bfars[0:1, 0:16], rowb[0:1, 48:64], ALU.subtract))
    p.op("vector", lambda v: v.tensor_tensor(crow[0:1, 16:32], bfars[0:1, 16:32], rowb[0:1, 48:64], ALU.subtract))
    cv = p.op("vector", lambda v: v.tensor_scalar(crow[0:1, 32:48], rowb[0:1, 48:64], -1.0, None, ALU.mult), sem="vv")
    p.wait("tensor", "vv", cv)
    pnc = es1.enter_context(nc.psum_tensor("pnc", [128, 3 * H], F32))
    gcb = p.op("tensor", lambda t: t.matmul(pnc[:], onesr[:], crow[0:1, :], start=True, stop=True), sem="tt")
    p.wait("vector", "tt", gcb)
    CBC_GG = p.op("vector", lambda v: v.tensor_copy(cbc[:], pnc[:]), sem="vv")

    if debug >= 2:
        p.wait("sync", "ss", p.cnt["ss"])
        stage1 = sb(es1, "dstage1", [128, NC * S // 4], F32)
        for nm, src in [("qT", qT), ("kT", kT)]:
            for hf in range(4):
                half = NC * S // 4
                vs = p.op("vector", lambda v, src=src, hf=hf, half=half: v.tensor_copy(
                    stage1[:], src[:].rearrange("p c s -> p (c s)")[:, hf * half:(hf + 1) * half]), sem="vv")
                p.wait("sync", "vv", vs)
                sd = p.dma("sync", dbg[nm][:].rearrange("p c s -> p (c s)")[:, hf * half:(hf + 1) * half],
                           stage1[:], "st")
                p.wait("vector", "st", sd)
        for hf in range(6):
            hw_ = NT * H * 66 // 6
            vs = p.op("vector", lambda v, hf=hf, hw_=hw_: v.tensor_copy(
                stage1[:, 0:hw_], vt[:].rearrange("p n h w -> p (n h w)")[:, hf * hw_:(hf + 1) * hw_]), sem="vv")
            p.wait("sync", "vv", vs)
            sd = p.dma("sync", dbg["vt"][:].rearrange("p n h w -> p (n h w)")[:, hf * hw_:(hf + 1) * hw_],
                       stage1[:, 0:hw_], "st")
            p.wait("vector", "st", sd)
        p.wait("gpsimd", "vv", CBC_GG)
        p.dma("gpsimd", dbg["cbc"][:], cbc[:], "gst")
        p.wait("gpsimd", "vv", MC)
        p.dma("gpsimd", dbg["mcol"][:], mcol[:], "gst")
    es1.close()
    if upto < 2:
        return _finish(nc, p, [es5, esP])

    rsxb = sb(es3, "rsxb", [128, S], F32)
    macc = sb(es3, "macc", [128, S], F32)
    wjit = [sb(es2, f"wjit{i}", [128, NC, 128], F32) for i in range(2)]
    wdjit = sb(es2, "wdjit", [128, 1024], F32)
    wq16 = sb(es2, "wq16", [128, 1024], F16)
    wqs = sb(es2, "wqs", [128, 1024], F32)
    wqg = [[sb(es2, f"wqg{par}{gi}", [128, NC, 128], BF16) for gi in range(2)] for par in range(2)]
    asb = sb(es2, "asb", [128, 1024], F32)
    wdqb = sb(es2, "wdqb", [128, 1024], BF16)
    halfc = sb(es2, "halfc", [128, 2], F32)
    gbuf = sb(es2, "gbuf", [128, 1024], F32)
    silb = sb(es2, "silb", [128, 1024], F32)
    habs = sb(es2, "habs", [128, 1024], F32)
    hbuf = [sb(es2, f"hbuf{i}", [128, 1024], F32) for i in range(2)]
    pg = [es2.enter_context(nc.psum_tensor(f"pg{i}", [128, 1024], F32)) for i in range(2)]
    pu = [es2.enter_context(nc.psum_tensor(f"pu{i}", [128, 1024], F32)) for i in range(2)]

    p.wait("sync", "tt", p.cnt["tt"])
    p.wait("sync", "ss", p.cnt["ss"])
    p.wait("sync", "vv", p.cnt["vv"])
    rg = p.dma("sync", sxr_scr[:], sxr_tm[:], "tp")
    p.wait("sync", "tp", rg)
    rg = p.dma("sync", macc[0:1, :], sxr_scr.rearrange("p n -> n p").unsqueeze(0), "tp",
               allow_slow_non_contiguous=True)
    p.wait("tensor", "ss", QKV_SS)
    p.wait("tensor", "vv", NORM_VV)
    p.wait("tensor", "tp", p.cnt["tp"])
    for qs in range(4):
        p.op("tensor", lambda t, qs=qs: t.matmul(
            pg[qs // 2][:, (qs % 2) * 512:(qs % 2 + 1) * 512], onesr[:],
            macc[0:1, qs * 512:(qs + 1) * 512], start=True, stop=True),
            sem="tt" if qs == 3 else None)
    grb = p.cnt["tt"]
    p.wait("vector", "tt", grb)
    for qs in range(4):
        p.op("vector", lambda v, qs=qs: v.tensor_copy(
            rsxb[:, qs * 512:(qs + 1) * 512], pg[qs // 2][:, (qs % 2) * 512:(qs % 2 + 1) * 512]))
    MACC0 = p.op("vector", lambda v: v.memset(macc[:], 0.0), sem="vv")

    if debug:
        p.wait("sync", "vv", MACC0)
        p.dma("sync", dbg["rsxb"][:], rsxb[:], "st")

    p.op("vector", lambda v: v.memset(halfc[:, 0:1], -0.5))
    HC = p.op("vector", lambda v: v.memset(halfc[:, 1:2], 0.5), sem="vv")
    p.wait("scalar", "vv", HC)
    p.wait("tensor", "ss", QKV_SS)
    p.wait("tensor", "vv", NORM_VV)
    p.wait("vector", "vv", MC)
    p.wait("scalar", "vv", MC)

    wgT_cols = wgT_h.rearrange("(c p) (b j) -> b p c j", p=128, j=128)
    wuT_cols = wuT_h.rearrange("(c p) (b j) -> b p c j", p=128, j=128)
    wdT_rows = wdT_h.rearrange("(b j) o -> b j o", j=128)
    wdq_rows = wdq_scr.rearrange("(b j) o -> b j o", j=128)
    h_rows = h_scr.rearrange("(b j) s -> b j s", j=128)

    ldq = {}
    hv_hist = []
    dh_hist = []
    wd_vv = []
    wd_st = []
    mm_tt = {}
    qfree = [0]
    ssil_l = [0]

    def quant_block(ib):
        if ib >= 2:
            p.wait("vector", "tt", mm_tt[ib - 2])
        for gi, wcols in enumerate([wgT_cols, wuT_cols]):
            buf = wjit[gi]
            if (ib - 1, gi) in ldq:
                p.wait("sync", "vv", ldq[(ib - 1, gi)])
            lane = "lw0" if gi == 0 else "lw1"
            lt = p.dma("sync", buf[:], wcols[ib], lane)
            p.wait("scalar", lane, lt)
            rcol = mcol[:, 0:1] if gi == 0 else mcol[:, 1:2]
            p.wait("scalar", "vv", qfree[0])
            p.op("scalar", lambda sc, buf=buf, rcol=rcol: sc.activation(
                wqs[:], buf[:].rearrange("p c j -> p (c j)"), AF.Sign, bias=halfc[:, 0:1], scale=rcol))
            s2 = p.op("scalar", lambda sc, buf=buf, rcol=rcol: sc.activation(
                habs[:], buf[:].rearrange("p c j -> p (c j)"), AF.Sign, bias=halfc[:, 1:2], scale=rcol), sem="ss")
            p.wait("vector", "ss", s2)
            ldq[(ib, gi)] = p.op("vector", lambda v, gi=gi, ib=ib: v.tensor_tensor(
                wqg[ib % 2][gi][:].rearrange("p c j -> p (c j)"), wqs[:], habs[:], ALU.add), sem="vv")
            qfree[0] = ldq[(ib, gi)]
        if wd_vv:
            p.wait("sync", "vv", wd_vv[-1])
        lt = p.dma("sync", wdjit[:], wdT_rows[ib], "lw2")
        p.wait("scalar", "lw2", lt)
        if wd_st:
            p.wait("vector", "fs0", wd_st[-1])
        p.wait("scalar", "vv", qfree[0])
        p.op("scalar", lambda sc: sc.activation(wqs[:], wdjit[:], AF.Sign, bias=halfc[:, 0:1], scale=mcol[:, 2:3]))
        s2 = p.op("scalar", lambda sc: sc.activation(habs[:], wdjit[:], AF.Sign, bias=halfc[:, 1:2], scale=mcol[:, 2:3]), sem="ss")
        p.wait("vector", "ss", s2)
        vwd = p.op("vector", lambda v: v.tensor_tensor(wdqb[:], wqs[:], habs[:], ALU.add), sem="vv")
        qfree[0] = vwd
        wd_vv.append(vwd)
        p.wait("sync", "vv", vwd)
        wd_st.append(p.dma("sync", wdq_rows[ib], wdqb[:], "fs0"))

    quant_block(0)
    for ib in range(NJ):
        if ib + 1 < NJ:
            quant_block(ib + 1)
        for th in range(2):
            pgt, put = pg[th], pu[th]
            p.wait("tensor", "vv", ldq[(ib, 1)])
            if len(hv_hist) >= 2:
                p.wait("tensor", "vv", hv_hist[-2])
            for gi, (ppp, wsrc) in enumerate([(pgt, wqg[ib % 2][0]), (put, wqg[ib % 2][1])]):
                for ic in range(NC):
                    for hf in range(2):
                        p.op("tensor", lambda t, ppp=ppp, wsrc=wsrc, ic=ic, th=th, hf=hf: t.matmul(
                            ppp[:, hf * 512:(hf + 1) * 512],
                            wsrc[:, ic, :],
                            xqT[:, th * 8 + hf * 4:th * 8 + (hf + 1) * 4, ic, :],
                            start=(ic == 0), stop=(ic == NC - 1)),
                            sem="tt" if (ic == NC - 1 and hf == 1 and gi == 1) else None)
            tv = p.cnt["tt"]
            mm_tt[ib] = tv
            p.wait("vector", "tt", tv)
            p.wait("vector", "ss", ssil_l[0])
            vg = p.op("vector", lambda v, pgt=pgt, th=th: v.scalar_tensor_tensor(
                gbuf[:], pgt[:], mcol[:, 3:4], rsxb[:, th * 1024:(th + 1) * 1024],
                ALU.mult, ALU.mult), sem="vv")
            p.wait("scalar", "vv", vg)
            ssil = p.op("scalar", lambda sc: sc.activation(silb[:], gbuf[:], AF.Silu), sem="ss")
            ssil_l[0] = ssil
            p.wait("vector", "ss", ssil)
            hb = hbuf[(ib * 2 + th) % 2]
            hlane = "dh0" if (ib * 2 + th) % 2 == 0 else "dh1"
            if len(dh_hist) >= 2:
                p.wait("vector", hlane, dh_hist[-2])
            vh = p.op("vector", lambda v, put=put, hb=hb: v.tensor_tensor(hb[:], silb[:], put[:], ALU.mult), sem="vv")
            hv_hist.append(vh)
            p.wait("scalar", "vv", vh)
            sab = p.op("scalar", lambda sc, hb=hb: sc.activation(asb[:], hb[:], AF.Abs), sem="ss")
            p.wait("vector", "ss", sab)
            p.op("vector", lambda v, th=th: v.tensor_max(
                macc[:, th * 1024:(th + 1) * 1024], macc[:, th * 1024:(th + 1) * 1024], asb[:]))
            p.wait("sync", "vv", vh)
            dh_hist.append(p.dma("sync", h_rows[ib][:, th * 1024:(th + 1) * 1024], hb[:], hlane))
    P2_DH0, P2_DH1 = p.cnt["dh0"], p.cnt["dh1"]

    sbcb, ebcb = macc, rsxb
    shsem = p.cnt["vv"]
    for half in range(2):
        cs_, ce_ = half * 1024, (half + 1) * 1024
        n = 64
        while n >= 1:
            p.wait("sync", "vv", p.cnt["vv"])
            sh = p.dma("sync", hbuf[half][0:n, 0:1024] if n > 0 else None,
                       macc[n:2 * n, cs_:ce_], "tp")
            p.wait("vector", "tp", sh)
            p.op("vector", lambda v, n=n, cs_=cs_, ce_=ce_, half=half: v.tensor_max(
                macc[0:n, cs_:ce_], macc[0:n, cs_:ce_], hbuf[half][0:n, 0:1024]),
                sem="vv")
            n //= 2
    MXV = p.cnt["vv"]
    p.wait("tensor", "vv", MXV)
    for qs in range(4):
        p.op("tensor", lambda t, qs=qs: t.matmul(
            pg[qs // 2][:, (qs % 2) * 512:(qs % 2 + 1) * 512], onesr[:],
            macc[0:1, qs * 512:(qs + 1) * 512], start=True, stop=True),
            sem="tt" if qs == 3 else None)
    gpr = p.cnt["tt"]
    p.wait("vector", "tt", gpr)
    for qs in range(4):
        p.op("vector", lambda v, qs=qs: v.tensor_copy(
            macc[:, qs * 512:(qs + 1) * 512], pg[qs // 2][:, (qs % 2) * 512:(qs % 2 + 1) * 512]))
    p.op("vector", lambda v: v.tensor_scalar(macc[:], macc[:], 1e-12, None, ALU.max))
    p.op("vector", lambda v: v.tensor_tensor(rsxb[:], rsxb[:], macc[:], ALU.mult))
    veb = p.op("vector", lambda v: v.tensor_scalar(rsxb[:], rsxb[:], mcol[:, 5:6], None, ALU.mult), sem="vv")
    p.op("vector", lambda v: v.reciprocal(hbuf[0][:], macc[:, 0:1024]))
    p.op("vector", lambda v: v.reciprocal(hbuf[1][:], macc[:, 1024:2048]))
    p.op("vector", lambda v: v.tensor_scalar(macc[:, 0:1024], hbuf[0][:], 127.0, None, ALU.mult))
    veb = p.op("vector", lambda v: v.tensor_scalar(macc[:, 1024:2048], hbuf[1][:], 127.0, None, ALU.mult), sem="vv")

    if debug >= 3:
        xq2stage = sb(es2, "xq2stage", [128, 4096], F32)
        p.wait("vector", "tt", p.cnt["tt"])
        for hf in range(4):
            half = NC * S // 4
            vs = p.op("vector", lambda v, hf=hf, half=half: v.tensor_copy(
                xq2stage[:], xqT[:].rearrange("p n c t -> p (n c t)")[:, hf * half:(hf + 1) * half]), sem="vv")
            p.wait("sync", "vv", vs)
            sd = p.dma("sync", dbg["xq2"][:].rearrange("p c s -> p (c s)")[:, hf * half:(hf + 1) * half],
                       xq2stage[:], "st")
            p.wait("vector", "st", sd)
        p.wait("sync", "dh0", P2_DH0)
        p.wait("sync", "dh1", P2_DH1)
        p.wait("sync", "vv", p.cnt["vv"])
        hsv = dbg["hscr"].rearrange("(b j) s -> b j s", j=128)
        hstage = xq2stage
        for jb in range(NJ):
            ldh = p.dma("sync", hstage[:, 0:2048], h_rows[jb], "ldw")
            p.wait("sync", "ldw", ldh)
            p.dma("sync", hsv[jb], hstage[:, 0:2048], "st")
        p.dma("sync", dbg["sbc"][:], sbcb[:], "st")
        p.dma("sync", dbg["ebc"][:], ebcb[:], "st")
        p.wait("sync", "st", p.cnt["st"])
    es2.close()
    if upto < 3:
        return _finish(nc, p, [es5, esP])

    fstage = [sb(es3, f"fstage{i}", [128, 1024], F32) for i in range(2)]
    hjit = [sb(es3, f"hjit{i}", [128, 512], F32) for i in range(3)]
    hi32 = sb(es3, "hi32", [128, 512], I32)
    hrb = [sb(es3, f"hrb{i}", [128, 512], BF16) for i in range(2)]
    hc1 = sb(es3, "hc1", [128, 512], BF16)
    hqm = [sb(es3, f"hqm{i}", [128, 512], F32) for i in range(2)]
    hqb = [sb(es3, f"hqb{i}", [128, 512], BF16) for i in range(2)]
    wdl = [sb(es3, f"wdl{i}", [128, 1024], BF16) for i in range(3)]
    pd = [es3.enter_context(nc.psum_tensor(f"pd{i}", [128, 512], F32)) for i in range(8)]
    ffn_rows = ffn_scr.rearrange("(c o) s -> c o s", o=128)

    p.wait("scalar", "dh0", P2_DH0)
    p.wait("scalar", "dh1", P2_DH1)
    p.wait("scalar", "vv", p.cnt["vv"])
    p.wait("sync", "fs0", p.cnt["fs0"])
    p.wait("sync", "tt", p.cnt["tt"])
    p.wait("sync", "ss", p.cnt["ss"])
    p.wait("sync", "vv", p.cnt["vv"])
    p.wait("vector", "vv", veb)
    p.wait("tensor", "vv", veb)
    dn_at = {}
    vq_at = {}
    lw_at = {}
    sh2_at = {}
    yv_at = {}
    fd_hist = []
    vf_last = 0
    NIT = 4 * NJ

    def qa(it):
        tq, jc = divmod(it, NJ)
        pr, r3 = it % 2, it % 3
        if it >= 3:
            p.wait("scalar", "vv", vq_at[it - 3])
            p.wait("sync", "dn", dn_at[it - 3])
        hlane = "dh0" if pr == 0 else "dh1"
        lh = p.dma("scalar", hjit[r3][:], h_rows[jc][:, tq * 512:(tq + 1) * 512], hlane)
        lw_at[it] = p.dma("sync", wdl[r3][:], wdq_rows[jc], "lw0" if pr == 0 else "lw1")
        p.wait("vector", hlane, lh)
        if it >= 2:
            p.wait("vector", "ss", sh2_at[it - 2])
        yv_at[it] = p.op("vector", lambda v, r3=r3, pr=pr, tq=tq: v.tensor_tensor(
            hqm[pr][:], hjit[r3][:], sbcb[:, tq * 512:(tq + 1) * 512], ALU.mult), sem="vv")
        p.wait("scalar", "vv", yv_at[it])
        p.op("scalar", lambda sc, pr=pr: sc.activation(hi32[:], hqm[pr][:], AF.Copy, bias=-1.0, scale=1.0))
        sh2_at[it] = p.op("scalar", lambda sc, pr=pr: sc.activation(
            hrb[pr][:], hi32[:], AF.Copy, bias=0.0, scale=1.0), sem="ss")

    def qb(it):
        pr = it % 2
        p.wait("vector", "ss", sh2_at[it])
        if it >= 2:
            p.wait("vector", "dn", dn_at[it - 2])
        p.op("vector", lambda v, pr=pr: v.tensor_tensor(hqm[pr][:], hqm[pr][:], hrb[pr][:], ALU.subtract))
        p.op("vector", lambda v, pr=pr: v.scalar_tensor_tensor(
            hc1[:], hqm[pr][:], 0.5, hrb[pr][:], ALU.is_ge, ALU.add))
        vq_at[it] = p.op("vector", lambda v, pr=pr: v.scalar_tensor_tensor(
            hqb[pr][:], hqm[pr][:], 1.5, hc1[:], ALU.is_ge, ALU.add), sem="vv")

    qa(0)
    qa(1)
    qb(0)
    for tq in range(4):
        for jc in range(NJ):
            it = tq * NJ + jc
            bi = it % 2
            if it + 2 < NIT:
                qa(it + 2)
            if it + 1 < NIT:
                qb(it + 1)
            p.wait("tensor", "vv", vq_at[it])
            p.wait("tensor", "lw0" if bi == 0 else "lw1", lw_at[it])
            if jc == 0 and tq > 0:
                p.wait("tensor", "vv", vf_last)
            for ob in range(8):
                p.op("tensor", lambda t, ob=ob, jc=jc, it=it: t.matmul(
                    pd[ob][:], wdl[it % 3][:].rearrange("j (a o) -> j a o", o=128)[:, ob, :],
                    hqb[it % 2][:],
                    start=(jc == 0), stop=(jc == NJ - 1)),
                    sem="dn" if ob == 7 else None)
            dn_at[it] = p.cnt["dn"]
        p.wait("vector", "dn", p.cnt["dn"])
        for ob in range(8):
            fs = fstage[ob % 2]
            flane = "fs0" if ob % 2 == 0 else "fs1"
            if len(fd_hist) >= 2:
                p.wait("vector", fd_hist[-2][1], fd_hist[-2][0])
            vf = p.op("vector", lambda v, ob=ob, tq=tq, fs=fs: v.tensor_tensor(
                fs[:, 0:512], pd[ob][:], ebcb[:, tq * 512:(tq + 1) * 512], ALU.mult), sem="vv")
            vf_last = vf
            p.wait("sync", "vv", vf)
            fd_hist.append((p.dma("sync", ffn_rows[ob][:, tq * 512:(tq + 1) * 512], fs[:, 0:512], flane), flane))

    if debug >= 4:
        p.wait("sync", "fs0", p.cnt["fs0"])
        p.wait("sync", "fs1", p.cnt["fs1"])
        fsv = dbg["ffn"].rearrange("(c o) s -> c o s", o=128)
        fstg = sb(es3, "fstg", [128, 2048], F32)
        for c in range(NC):
            ldf = p.dma("sync", fstg[:], ffn_rows[c], "ldw")
            p.wait("sync", "ldw", ldf)
            p.dma("sync", fsv[c], fstg[:], "st")
        p.wait("sync", "st", p.cnt["st"])
    es3.close()
    if upto < 4:
        return _finish(nc, p, [es5, esP])

    es4a = ExitStack()
    attnT = sb(es4, "attnT", [128, NC, S], BF16)
    bv2t = sb(es4a, "bv2t_s", [128, H, BW], BF16)
    ring = sb(es4a, "ring", [128, 3, 1024], BF16)
    att_u = sb(es4a, "att_u", [64, 1024], BF16)
    denst = sb(es4a, "denst", [128, 1024], F32)
    ps_s = [es4a.enter_context(nc.psum_tensor(f"ps_s{i}", [128, 1024], F32)) for i in range(2)]
    pav = [es4a.enter_context(nc.psum_tensor(f"pav{i}", [65, 1024], F32)) for i in range(2)]

    p.wait("gpsimd", "st", p.cnt["st"])
    p.wait("gpsimd", "fs0", p.cnt["fs0"])
    p.wait("gpsimd", "fs1", p.cnt["fs1"])
    p.wait("gpsimd", "dn", p.cnt["dn"])
    p.wait("gpsimd", "vv", p.cnt["vv"])
    lbv = p.dma("gpsimd", bv2t[:],
                bv2t_h.rearrange("h p w -> p h w"), "ldg")
    p.wait("tensor", "ldg", lbv)
    p.wait("tensor", "vv", CBC_GG)
    p.wait("tensor", "vv", vf_last)
    p.wait("scalar", "vv", CBC_GG)

    exp_at = {}
    avkey = {}
    evdone = {}
    den_dma = []
    odd_dma = []

    def _av_step(h, qm, j, pavt):
        p.wait("tensor", "ex", exp_at[(h, qm, j)])
        for qh in range(2):
            p.op("tensor", lambda t, h=h, j=j, qh=qh, pavt=pavt: t.matmul(
                pavt[:, qh * 512:(qh + 1) * 512],
                vt[:, j, h, 0:65],
                ring[:, j % 3, qh * 512:(qh + 1) * 512],
                start=(j == 0), stop=(j == NT - 1)),
                sem="av" if qh == 1 else None)
        avkey[(h, qm, j)] = p.cnt["av"]
    for h in range(H):
        for qm in range(2):
            pavt = pav[qm]
            if h >= 1:
                prevev = evdone[(h - 1, qm)]
                if isinstance(prevev, tuple):
                    p.wait("tensor", prevev[0], prevev[1])
                else:
                    p.wait("tensor", "dv", prevev)
            base = (h % 2) * 64
            for kc in range(NT):
                pst = ps_s[kc % 2]
                if kc >= 2:
                    p.wait("tensor", "ex", exp_at[(h, qm, kc - 2)])
                elif (h, qm) != (0, 0):
                    pk = (h, qm - 1) if qm == 1 else (h - 1, 1)
                    p.wait("tensor", "ex", exp_at[(pk[0], pk[1], NT - 2 + kc)])
                kinds = []
                for qh in range(2):
                    q0 = qm * 1024 + qh * 512
                    kind, arg = _near(kc, q0 // 512)
                    kinds.append((kind, arg))
                    p.op("tensor", lambda t, kc=kc, q0=q0, qh=qh, pst=pst, base=base, h=h,
                         stop=(kind == "far"): t.matmul(
                        pst[:, qh * 512:(qh + 1) * 512],
                        kT[base:base + 64, h // 2, kc * 128:(kc + 1) * 128],
                        qT[base:base + 64, h // 2, q0:q0 + 512],
                        start=True, stop=stop),
                        sem="qk" if (qh == 1 and kind == "far") else None)
                    if kind == "near":
                        p.op("tensor", lambda t, h=h, arg=arg, qh=qh, pst=pst: t.matmul(
                            pst[:, qh * 512:(qh + 1) * 512], idmb[:],
                            bv2t[:, h, arg:arg + 512],
                            start=False, stop=True),
                            sem="qk" if qh == 1 else None)
                qkv_ = p.cnt["qk"]
                p.wait("scalar", "qk", qkv_)
                slot = kc % 3
                if kc >= 3:
                    p.wait("scalar", "av", avkey[(h, qm, kc - 3)])
                elif (h, qm) != (0, 0):
                    pk = (h, qm - 1) if qm == 1 else (h - 1, 1)
                    p.wait("scalar", "av", avkey[(pk[0], pk[1], NT - 3 + kc)])
                (k0, a0), (k1, a1) = kinds
                c0 = (a0 if k0 == "far" else 2) * H + h
                c1 = (a1 if k1 == "far" else 2) * H + h
                if c0 == c1:
                    exp_at[(h, qm, kc)] = p.op("scalar", lambda sc, pst=pst, slot=slot, c0=c0: sc.activation(
                        ring[:, slot, :], pst[:], AF.Exp, bias=cbc[:, c0:c0 + 1], scale=1.0), sem="ex")
                else:
                    for qh, cc in [(0, c0), (1, c1)]:
                        exp_at[(h, qm, kc)] = p.op("scalar", lambda sc, pst=pst, qh=qh, slot=slot, cc=cc: sc.activation(
                            ring[:, slot, qh * 512:(qh + 1) * 512], pst[:, qh * 512:(qh + 1) * 512],
                            AF.Exp, bias=cbc[:, cc:cc + 1], scale=1.0), sem="ex")
                if kc >= 1:
                    _av_step(h, qm, kc - 1, pavt)
            _av_step(h, qm, NT - 1, pavt)
            p.wait("vector", "av", avkey[(h, qm, NT - 1)])
            if den_dma:
                p.wait("vector", "dd", den_dma[-1])
            dvc = p.op("vector", lambda v, pavt=pavt: v.tensor_copy(denst[64:65, :], pavt[64:65, :]), sem="dv")
            p.wait("sync", "dv", dvc)
            den_dma.append(p.dma("sync", den_scr[0:1, (h * 2 + qm) * 1024:(h * 2 + qm + 1) * 1024],
                                 denst[64:65, :], "dd"))
            if h % 2 == 0:
                ev_ = p.op("vector", lambda v, h=h, qm=qm, pavt=pavt: v.tensor_copy(
                    attnT[0:64, h // 2, qm * 1024:(qm + 1) * 1024], pavt[0:64, :]), sem="dv")
            else:
                if odd_dma:
                    p.wait("vector", "do", odd_dma[-1])
                ev0 = p.op("vector", lambda v, pavt=pavt: v.tensor_copy(att_u[:, :], pavt[0:64, :]), sem="dv")
                p.wait("gpsimd", "dv", ev0)
                odd_dma.append(p.dma("gpsimd", attnT[64:128, h // 2, qm * 1024:(qm + 1) * 1024], att_u[:, :], "do"))
                ev_ = ("do", odd_dma[-1])
            evdone[(h, qm)] = ev_
    P4_DV = p.cnt["dv"]

    es4a.close()
    dens2 = sb(es4, "dens2", [128, 256], F32)
    rcpr = [sb(es4, f"rcpr{i}", [1, S], F32) for i in range(2)]
    pdv = es4.enter_context(nc.psum_tensor("pdv", [128, S], F32))
    p.wait("sync", "dv", P4_DV)
    p.wait("sync", "dd", p.cnt["dd"])
    p.wait("sync", "do", p.cnt["do"])
    p.wait("tensor", "dv", P4_DV)
    p.wait("tensor", "dd", p.cnt["dd"])
    p.wait("tensor", "do", p.cnt["do"])
    ld_d = p.dma("sync", dens2[:], den_scr[0:1, :].rearrange("o (q c) -> (o q) c", c=256), "ldw")
    p.wait("vector", "ldw", ld_d)
    vr = p.op("vector", lambda v: v.reciprocal(dens2[:], dens2[:]), sem="vv")
    p.wait("sync", "vv", vr)
    wr_d = p.dma("sync", rcp_scr[0:1, :].rearrange("o (q c) -> (o q) c", c=256), dens2[:], "dv")
    p.wait("sync", "dv", p.cnt["dv"])
    DIV_VV = 0
    for c in range(NC):
        lv0 = p.dma("sync", rcpr[0][:], rcp_scr[0:1, (2 * c) * S:(2 * c + 1) * S], "ldw")
        lv = p.dma("sync", rcpr[1][:], rcp_scr[0:1, (2 * c + 1) * S:(2 * c + 2) * S], "ldw")
        p.wait("tensor", "ldw", lv)
        if DIV_VV:
            p.wait("tensor", "vv", DIV_VV)
        for qs in range(4):
            p.op("tensor", lambda t, qs=qs: t.matmul(
                pdv[0:64, qs * 512:(qs + 1) * 512], onesr[0:1, 0:64],
                rcpr[0][0:1, qs * 512:(qs + 1) * 512], start=True, stop=True))
            p.op("tensor", lambda t, qs=qs: t.matmul(
                pdv[64:128, qs * 512:(qs + 1) * 512], onesr[0:1, 0:64],
                rcpr[1][0:1, qs * 512:(qs + 1) * 512], start=True, stop=True,
                tile_position=(0, 64)),
                sem="tt" if qs == 3 else None)
        dvt = p.cnt["tt"]
        p.wait("vector", "tt", dvt)
        p.op("vector", lambda v, c=c: v.tensor_tensor(
            attnT[:, c, 0:1024], attnT[:, c, 0:1024], pdv[:, 0:1024], ALU.mult))
        DIV_VV = p.op("vector", lambda v, c=c: v.tensor_tensor(
            attnT[:, c, 1024:2048], attnT[:, c, 1024:2048], pdv[:, 1024:2048], ALU.mult), sem="vv")
        p.wait("sync", "vv", DIV_VV)

    if debug >= 5:
        stg4 = sb(es4, "dstage4", [128, NC * S // 4], F32)
        for hf in range(4):
            half = NC * S // 4
            vs = p.op("vector", lambda v, hf=hf, half=half: v.tensor_copy(
                stg4[:], attnT[:].rearrange("p c s -> p (c s)")[:, hf * half:(hf + 1) * half]), sem="vv")
            p.wait("sync", "vv", vs)
            sd = p.dma("sync", dbg["attnT"][:].rearrange("p c s -> p (c s)")[:, hf * half:(hf + 1) * half],
                       stg4[:], "st")
            p.wait("vector", "st", sd)

    wob = sb(es4, "wob", [128, NC, D], BF16)
    fstage = [sb(es4, f"fstage5{i}", [128, 1024], F32) for i in range(2)]
    p.wait("gpsimd", "ex", p.cnt["ex"])
    p.wait("gpsimd", "av", p.cnt["av"])
    p.wait("gpsimd", "vv", p.cnt["vv"])
    lwo = p.dma("gpsimd", wob[:],
                woT_h.rearrange("(c p) d -> p c d", p=128), "ldg")
    pw = es4.enter_context(nc.psum_tensor("pw", [128, 1024], F32))
    att_rows = att_scr.rearrange("(c o) s -> c o s", o=128)
    p.wait("tensor", "ldg", lwo)
    p.wait("tensor", "vv", DIV_VV)
    vw_hist = []
    aw_hist = []
    for oc in range(NC):
        for th in range(2):
            if vw_hist:
                p.wait("tensor", "vv", vw_hist[-1])
            for ic in range(NC):
                for hf in range(2):
                    p.op("tensor", lambda t, oc=oc, ic=ic, th=th, hf=hf: t.matmul(
                        pw[:, hf * 512:(hf + 1) * 512],
                        wob[:, ic, oc * 128:(oc + 1) * 128],
                        attnT[:, ic, th * 1024 + hf * 512:th * 1024 + (hf + 1) * 512],
                        start=(ic == 0), stop=(ic == NC - 1)),
                        sem="tt" if (ic == NC - 1 and hf == 1) else None)
            tv = p.cnt["tt"]
            p.wait("vector", "tt", tv)
            fs = fstage[(oc * 2 + th) % 2]
            flane = "fs0" if (oc * 2 + th) % 2 == 0 else "fs1"
            if len(aw_hist) >= 2:
                p.wait("vector", aw_hist[-2][1], aw_hist[-2][0])
            vw = p.op("vector", lambda v, fs=fs: v.tensor_copy(fs[:], pw[:]), sem="vv")
            vw_hist.append(vw)
            p.wait("sync", "vv", vw)
            aw_hist.append((p.dma("sync", att_rows[oc][:, th * 1024:(th + 1) * 1024], fs[:], flane), flane))

    if debug >= 5:
        p.wait("sync", "fs0", p.cnt["fs0"])
        p.wait("sync", "fs1", p.cnt["fs1"])
        asv = dbg["att"].rearrange("(c o) s -> c o s", o=128)
        astg = sb(es4, "astg", [128, 2048], F32)
        for c in range(NC):
            lda = p.dma("sync", astg[:], att_rows[c], "ldw")
            p.wait("sync", "ldw", lda)
            p.dma("sync", asv[c], astg[:], "st")
    es4.close()
    if upto < 5:
        return _finish(nc, p, [es5, esP])

    fchunk = [sb(es5, f"fchunk{i}", [128, NC, 128], F32) for i in range(2)]
    achunk = [sb(es5, f"achunk{i}", [128, NC, 128], F32) for i in range(2)]
    ostage = [sb(es5, f"ostage{i}", [128, D], F32) for i in range(2)]
    xtile = [sb(es5, f"xtile5{i}", [128, D], F32) for i in range(2)]
    po = [es5.enter_context(nc.psum_tensor(f"po{i}", [128, 1024], F32)) for i in range(2)]
    out_tiles = out_hbm.rearrange("(n p) d -> n p d", p=128)
    ffn_cv = ffn_scr.rearrange("(c p) s -> p c s", p=128)
    att_cv = att_scr.rearrange("(c p) s -> p c s", p=128)
    p.wait("sync", "fs0", p.cnt["fs0"])
    p.wait("sync", "fs1", p.cnt["fs1"])
    p.wait("scalar", "fs0", p.cnt["fs0"])
    p.wait("scalar", "fs1", p.cnt["fs1"])
    p.wait("tensor", "vv", p.cnt["vv"])
    odone = {}
    tdone = {}
    stly = []
    for tt_ in range(NT):
        bi = tt_ % 2
        if tt_ >= 2:
            p.wait("sync", "tt", tdone[tt_ - 2])
            p.wait("scalar", "vv", odone[tt_ - 2])
        clane = "lw0" if bi == 0 else "lw1"
        xlane = "lx0" if bi == 0 else "lx1"
        lf = p.dma("sync", fchunk[bi][:], ffn_cv[:, :, tt_ * 128:(tt_ + 1) * 128], clane)
        la = p.dma("sync", achunk[bi][:], att_cv[:, :, tt_ * 128:(tt_ + 1) * 128], clane)
        lx = p.dma("scalar", xtile[bi][:], x_tiles[tt_], xlane)
        p.wait("tensor", clane, la)
        if tt_ >= 2:
            p.wait("tensor", "vv", odone[tt_ - 2])
        pot = po[bi]
        for c in range(NC):
            p.op("tensor", lambda t, c=c, bi=bi, pot=pot, st=(c % 4 == 0): t.matmul(
                pot[:, c * 128:(c + 1) * 128], fchunk[bi][:, c, :], idmf[:],
                start=st, stop=False, is_transpose=True))
        for c in range(NC):
            p.op("tensor", lambda t, c=c, bi=bi, pot=pot, sp=(c % 4 == 3): t.matmul(
                pot[:, c * 128:(c + 1) * 128], achunk[bi][:, c, :], idmf[:],
                start=False, stop=sp, is_transpose=True),
                sem="tt" if c == NC - 1 else None)
        tdone[tt_] = p.cnt["tt"]
        p.wait("vector", "tt", tdone[tt_])
        p.wait("vector", xlane, lx)
        if tt_ >= 2:
            p.wait("vector", "so0" if tt_ % 2 == 0 else "so1", stly[tt_ - 2])
        odone[tt_] = p.op("vector", lambda v, bi=bi, pot=pot: v.tensor_tensor(
            ostage[bi][:], pot[:], xtile[bi][:], ALU.add), sem="vv")
        p.wait("sync", "vv", odone[tt_])
        stly.append(p.dma("sync", out_tiles[tt_], ostage[bi][:], "so0" if bi == 0 else "so1"))

    for s_ in ["st", "so0", "so1", "fs0", "fs1", "dd", "do", "dh0", "dh1", "gst"]:
        p.wait("sync", s_, p.cnt[s_])

    return _finish(nc, p, [es5, esP])


def _emit_guard(nc, p, stacks):
    return _finish(nc, p, stacks)


def _finish(nc, p, stacks):
    for s_, v_ in sorted(p.cnt.items()):
        if v_ > 0:
            p.wait("sync", s_, v_)
    sem_es = ExitStack()
    try:
        for name in SEMS:
            p.handles[name] = sem_es.enter_context(nc.semaphore(name))
        nc.all_engine_barrier()
        for name in SEMS:
            nc.gpsimd.sem_clear(p.handles[name])
        nc.all_engine_barrier()
        with nc.Block() as block:
            @block.sync
            def _(e):
                for f in p.ops["sync"]:
                    f(e)

            @block.gpsimd
            def _(e):
                for f in p.ops["gpsimd"]:
                    f(e)

            @block.vector
            def _(e):
                for f in p.ops["vector"]:
                    f(e)

            @block.scalar
            def _(e):
                for f in p.ops["scalar"]:
                    f(e)

            @block.tensor
            def _(e):
                for f in p.ops["tensor"]:
                    f(e)
    finally:
        sem_es.close()
        for st_ in stacks:
            try:
                st_.close()
            except Exception:
                pass

    return nc


def _host_inputs(inputs):
    x = np.ascontiguousarray(np.asarray(inputs["L0_attn_x"], dtype=np.float32))
    table = np.asarray(inputs["L0_attn_rel_bias_table"], dtype=np.float32)
    shared = {
        "wqT": np.ascontiguousarray(np.asarray(inputs["L0_attn_w_q"], np.float32).T),
        "wkT": np.ascontiguousarray(np.asarray(inputs["L0_attn_w_k"], np.float32).T),
        "wvT": np.ascontiguousarray(np.asarray(inputs["L0_attn_w_v"], np.float32).T),
        "woT": np.ascontiguousarray(np.asarray(inputs["L0_attn_w_o"], np.float32).T),
        "wgT": np.ascontiguousarray(np.asarray(inputs["L0_ffn_w_gate"], np.float32).T),
        "wuT": np.ascontiguousarray(np.asarray(inputs["L0_ffn_w_up"], np.float32).T),
        "wdT": np.ascontiguousarray(np.asarray(inputs["L0_ffn_w_down"], np.float32).T),
        "tabT": np.ascontiguousarray(table.T),
        "idm": np.eye(128, dtype=np.float32),
        "bfar": np.ascontiguousarray(
            np.concatenate([table[15, :], table[31, :]])[None, :]),
    }
    d = np.arange(-(S - 1), S)
    bv = table[_rel_bucket(d), :]
    j = np.arange(BW)
    pp = np.arange(128)
    idx = J0 + j[None, :] - pp[:, None]
    shared["bv2t"] = np.ascontiguousarray(bv[idx, :].transpose(2, 0, 1))
    in_maps = []
    for b in range(8):
        m = dict(shared)
        m["x"] = np.ascontiguousarray(x[b])
        in_maps.append(m)
    return in_maps


def kernel(**inputs):
    in_maps = _host_inputs(inputs)
    nc = build_program(debug=0)
    res = run_bass_kernel_spmd(nc, in_maps, core_ids=list(range(8)))
    return np.stack([r["out"] for r in res.results], axis=0).astype(np.float32)


# version 11 (final)
# speedup vs baseline: 1.0388x; 1.0388x over previous
import numpy as np
from contextlib import ExitStack

import concourse.bass as bass
import concourse.mybir as mybir
from concourse.bass_utils import run_bass_kernel_spmd

F32 = mybir.dt.float32
BF16 = mybir.dt.bfloat16
F16 = mybir.dt.float16
I32 = mybir.dt.int32
AX = mybir.AxisListType.X
AF = mybir.ActivationFunctionType
ALU = mybir.AluOpType

D, S, H, DK, INNER = 1024, 2048, 16, 64, 4096
NT, NC, NJ = S // 128, D // 128, INNER // 128
EPS = 1e-5
NUM_BUCKETS, MAX_DISTANCE = 32, 128
BW, J0 = 1344, 1445


def _rel_bucket(n):
    nb = NUM_BUCKETS // 2
    ret = (n < 0).astype(np.int64) * nb
    na = np.abs(n)
    mex = nb // 2
    large = mex + (np.log(np.maximum(na, 1) / mex)
                   / np.log(MAX_DISTANCE / mex) * (nb - mex)).astype(np.int64)
    large = np.minimum(large, nb - 1)
    return ret + np.where(na < mex, na, large)


def _near(kc, qb):
    delta = qb * 512 - kc * 128
    if delta >= 218:
        return ("far", 0)
    if delta <= -602:
        return ("far", 1)
    return ("near", delta + 602)


class Prog:
    ENGS = ("sync", "gpsimd", "vector", "scalar", "tensor")

    def __init__(self, sems):
        self.ops = {e: [] for e in self.ENGS}
        self.cnt = {n: 0 for n in sems}
        self.handles = {}

    def dma(self, eng, out, in_, sem, **kw):
        def f(e, out=out, in_=in_, sem=sem, kw=kw):
            e.dma_start(out, in_, **kw).then_inc(self.handles[sem], 16)
        self.ops[eng].append(f)
        self.cnt[sem] += 16
        return self.cnt[sem]

    def dmat(self, eng, out, in_, sem):
        def f(e, out=out, in_=in_, sem=sem):
            e.dma_start_transpose(out, in_).then_inc(self.handles[sem], 16)
        self.ops[eng].append(f)
        self.cnt[sem] += 16
        return self.cnt[sem]

    def wait(self, eng, sem, val):
        if val > 0:
            self.ops[eng].append(lambda e, s=sem, v=val: e.wait_ge(self.handles[s], v))

    def op(self, eng, fn, sem=None):
        def f(e, fn=fn, sem=sem):
            i = fn(e)
            if sem is not None:
                i.then_inc(self.handles[sem], 1)
        self.ops[eng].append(f)
        if sem is not None:
            self.cnt[sem] += 1
            return self.cnt[sem]
        return None

    def opw(self, eng, fn, sem="vv"):
        c = self.op(eng, fn, sem=sem)
        self.wait(eng, sem, c)
        return c


SEMS = ["ldx", "ldw", "ldg", "tp", "vv", "ss", "tt", "gg", "st", "dh",
        "qk", "ex", "av", "dv", "dn",
        "lx0", "lx1", "lm0", "lm1", "lw0", "lw1", "lw2",
        "dh0", "dh1", "fs0", "fs1", "dd", "do", "so0", "so1", "gst"]


class _Stop(Exception):
    pass


def build_program(debug=0, upto=99):
    nc = bass.Bass("TRN2", target_bir_lowering=False, detect_race_conditions=False)
    p = Prog(SEMS)

    x_hbm = nc.dram_tensor("x", [S, D], F32, kind="ExternalInput")
    wqT_h = nc.dram_tensor("wqT", [D, D], F32, kind="ExternalInput")
    wkT_h = nc.dram_tensor("wkT", [D, D], F32, kind="ExternalInput")
    wvT_h = nc.dram_tensor("wvT", [D, D], F32, kind="ExternalInput")
    woT_h = nc.dram_tensor("woT", [D, D], F32, kind="ExternalInput")
    wgT_h = nc.dram_tensor("wgT", [D, INNER], F32, kind="ExternalInput")
    wuT_h = nc.dram_tensor("wuT", [D, INNER], F32, kind="ExternalInput")
    wdT_h = nc.dram_tensor("wdT", [INNER, D], F32, kind="ExternalInput")
    bv2t_h = nc.dram_tensor("bv2t", [H, 128, BW], F32, kind="ExternalInput")
    bfar_h = nc.dram_tensor("bfar", [1, 2 * H], F32, kind="ExternalInput")
    tabT_h = nc.dram_tensor("tabT", [H, NUM_BUCKETS], F32, kind="ExternalInput")
    idm_h = nc.dram_tensor("idm", [128, 128], F32, kind="ExternalInput")
    out_hbm = nc.dram_tensor("out", [S, D], F32, kind="ExternalOutput")

    h_scr = nc.dram_tensor("h_scr", [INNER, S], F32, kind="Internal")
    wdq_scr = nc.dram_tensor("wdq_scr", [INNER, D], BF16, kind="Internal")
    ffn_scr = nc.dram_tensor("ffn_scr", [D, S], F32, kind="Internal")
    att_scr = nc.dram_tensor("att_scr", [D, S], F32, kind="Internal")
    den_scr = nc.dram_tensor("den_scr", [1, 2 * H * 1024], F32, kind="Internal")
    sxr_scr = nc.dram_tensor("sxr_scr", [128, NT], F32, kind="Internal")
    nrm_scr = nc.dram_tensor("nrm_scr", [2, 3, 8], F32, kind="Internal")
    tab_scr = nc.dram_tensor("tab_scr", [16, 1], F32, kind="Internal")
    rcp_scr = nc.dram_tensor("rcp_scr", [1, 2 * H * 1024], F32, kind="Internal")

    dbg = {}
    if debug:
        def dt_(name, shape):
            dbg[name] = nc.dram_tensor("d_" + name, shape, F32, kind="ExternalOutput")
        dt_("xnT", [128, NC, S]); dt_("xqT", [128, NC, S]); dt_("rsxb", [128, S])
        dbg["xntm"] = nc.dram_tensor("d_xntm", [128, NT, D], BF16, kind="ExternalOutput")
        dt_("lnr", [128, 8]); dt_("xt0", [128, D])
        dbg["e0"] = nc.dram_tensor("d_e0", [128, D], BF16, kind="ExternalOutput")
        dbg["e1"] = nc.dram_tensor("d_e1", [128, D], BF16, kind="ExternalOutput")
        if debug >= 2:
            dt_("qT", [128, NC, S]); dt_("kT", [128, NC, S])
            dt_("vt", [128, NT, H, 66]); dt_("cbc", [128, 3 * H]); dt_("mcol", [128, 8])
        if debug >= 3:
            dt_("hscr", [INNER, S]); dt_("sbc", [128, S]); dt_("ebc", [128, S])
            dt_("G0", [128, S]); dt_("U0", [128, S]); dt_("xq2", [128, NC, S]); dt_("wqg0", [128, NC, 128])
        if debug >= 4:
            dt_("ffn", [D, S])
        if debug >= 5:
            dt_("attnT", [128, NC, S]); dt_("att", [D, S])

    esP, es0, es1, es2, es3, es4, es5 = (ExitStack() for _ in range(7))

    def sb(es_, name, shape, dt, **kw):
        return es_.enter_context(nc.sbuf_tensor(name, shape, dt, **kw))

    idmb = sb(esP, "idmb", [128, 128], BF16)
    idmf = sb(esP, "idmf", [128, 128], F32)
    cbc = sb(esP, "cbc", [128, 3 * H], F32)
    sxr_tm = sb(esP, "sxr_tm", [128, NT], F32)
    mcol = sb(esP, "mcol", [128, 8], F32)
    lnrow = sb(esP, "lnrow", [128, 8], F32)
    onesr = sb(esP, "onesr", [1, 128], F32)

    xqT = sb(es2, "xqT", [128, NT, NC, 128], BF16, side="right", is_transpose_dest=True)
    xnT = sb(es1, "xnT", [128, NT, NC, 128], BF16, side="right", is_transpose_dest=True)
    xn_tm = sb(es0, "xn_tm", [128, NT, D], BF16)
    xq_tm = sb(es0, "xq_tm", [128, NT, D], BF16)
    xnf = [sb(es0, f"xnf{i}", [128, D], F32) for i in range(2)]
    yqb = sb(es0, "yqb", [128, D], F32)
    i32q = sb(es0, "i32q", [128, D], I32)
    rbb = sb(es0, "rbb", [128, D], BF16)
    cq1 = sb(es0, "cq1", [128, D], BF16)
    xtile = [sb(es0, f"xtile{i}", [128, D], F32) for i in range(2)]

    x_tiles = x_hbm.rearrange("(n p) d -> n p d", p=128)
    p.dma("gpsimd", idmb[:], idm_h[:], "ldg")
    p.dma("sync", idmf[:], idm_h[:], "ldw")

    vdone = []
    p.op("vector", lambda v: v.memset(onesr[:], 1.0))
    ldx = [p.dma("sync", xtile[0][:], x_tiles[0], "lx0"),
           p.dma("sync", xtile[1][:], x_tiles[1], "lx1")]
    for i in range(NT):
        xt = xtile[i % 2][:]
        xn32 = xnf[i % 2][:]
        lxs = "lx0" if i % 2 == 0 else "lx1"
        p.wait("vector", lxs, ldx[i])
        p.wait("scalar", lxs, ldx[i])
        if i >= 2:
            p.wait("scalar", "vv", vdone[i - 2])
        p.op("vector", lambda v, xt=xt: v.reduce_sum(lnrow[:, 0:1], xt, AX))
        sv = p.op("scalar", lambda sc, xt=xt, xn32=xn32: sc.activation(xn32, xt, AF.Square), sem="ss")
        p.wait("vector", "ss", sv)
        p.opw("vector", lambda v, xn32=xn32: v.reduce_sum(lnrow[:, 1:2], xn32, AX))
        p.opw("vector", lambda v: v.tensor_scalar(lnrow[:, 2:3], lnrow[:, 0:1], 1.0 / D, None, ALU.mult))
        p.opw("vector", lambda v: v.tensor_tensor(lnrow[:, 3:4], lnrow[:, 2:3], lnrow[:, 2:3], ALU.mult))
        p.opw("vector", lambda v: v.tensor_scalar(lnrow[:, 4:5], lnrow[:, 1:2], 1.0 / D, None, ALU.mult))
        p.opw("vector", lambda v: v.tensor_tensor(lnrow[:, 4:5], lnrow[:, 4:5], lnrow[:, 3:4], ALU.subtract))
        vv1 = p.op("vector", lambda v: v.tensor_scalar(lnrow[:, 4:5], lnrow[:, 4:5], EPS, None, ALU.add), sem="vv")
        p.wait("scalar", "vv", vv1)
        sv = p.op("scalar", lambda sc: sc.activation(lnrow[:, 5:6], lnrow[:, 4:5], AF.Sqrt), sem="ss")
        p.wait("vector", "ss", sv)
        p.opw("vector", lambda v: v.reciprocal(lnrow[:, 6:7], lnrow[:, 5:6]))
        p.opw("vector", lambda v: v.tensor_tensor(lnrow[:, 7:8], lnrow[:, 6:7], lnrow[:, 6:7], ALU.mult))
        p.opw("vector", lambda v: v.tensor_tensor(lnrow[:, 7:8], lnrow[:, 7:8], lnrow[:, 4:5], ALU.mult))
        p.opw("vector", lambda v: v.tensor_scalar(lnrow[:, 7:8], lnrow[:, 7:8], -0.5, 1.5, ALU.mult, ALU.add))
        p.opw("vector", lambda v: v.tensor_tensor(lnrow[:, 6:7], lnrow[:, 6:7], lnrow[:, 7:8], ALU.mult))
        p.opw("vector", lambda v: v.tensor_tensor(lnrow[:, 3:4], lnrow[:, 2:3], lnrow[:, 6:7], ALU.mult))
        p.opw("vector", lambda v: v.tensor_scalar(lnrow[:, 3:4], lnrow[:, 3:4], -1.0, None, ALU.mult))
        p.op("vector", lambda v, xt=xt, xn32=xn32: v.tensor_scalar(
            xn32, xt, lnrow[:, 6:7], lnrow[:, 3:4], ALU.mult, ALU.add))
        cpd = p.op("vector", lambda v, i=i, xn32=xn32: v.tensor_copy(xn_tm[:, i, :], xn32),
                   sem="vv" if (debug and i == 0) else None)
        if debug and i == 0:
            p.wait("sync", "vv", cpd)
            p.dma("sync", dbg["xt0"][:], xtile[0][:], "st")
            st0 = p.dma("sync", dbg["e0"][:], xn_tm[:, 0, :], "st")
            p.wait("sync", "st", st0)
        p.opw("vector", lambda v, xn32=xn32: v.reduce_max(lnrow[:, 0:1], xn32, AX, apply_absolute_value=True))
        p.opw("vector", lambda v: v.tensor_scalar(lnrow[:, 0:1], lnrow[:, 0:1], 1e-5, None, ALU.max))
        p.op("vector", lambda v, i=i: v.tensor_scalar(sxr_tm[:, i:i + 1], lnrow[:, 0:1], 1.0 / 127.0, None, ALU.mult))
        p.opw("vector", lambda v: v.reciprocal(lnrow[:, 1:2], lnrow[:, 0:1]))
        p.opw("vector", lambda v: v.tensor_scalar(lnrow[:, 1:2], lnrow[:, 1:2], 127.0, None, ALU.mult))
        yv_ = p.op("vector", lambda v, xn32=xn32: v.tensor_scalar(
            yqb[:], xn32, lnrow[:, 1:2], None, ALU.mult), sem="vv")
        p.wait("scalar", "vv", yv_)
        p.op("scalar", lambda sc: sc.activation(i32q[:], yqb[:], AF.Copy, bias=-1.0, scale=1.0))
        sq2 = p.op("scalar", lambda sc: sc.activation(rbb[:], i32q[:], AF.Copy, bias=0.0, scale=1.0), sem="ss")
        p.wait("vector", "ss", sq2)
        p.op("vector", lambda v: v.tensor_tensor(yqb[:], yqb[:], rbb[:], ALU.subtract))
        p.op("vector", lambda v: v.scalar_tensor_tensor(
            cq1[:], yqb[:], 0.5, rbb[:], ALU.is_ge, ALU.add))
        vdone.append(p.op("vector", lambda v, i=i: v.scalar_tensor_tensor(
            xq_tm[:, i, :], yqb[:], 1.5, cq1[:], ALU.is_ge, ALU.add), sem="vv"))
        p.wait("sync", "vv", vdone[i])
        if i + 2 < NT:
            ldx.append(p.dma("sync", xtile[i % 2][:], x_tiles[i + 2], lxs))
            p.wait("sync", lxs, ldx[-1])
        if i == 0:
            p.wait("sync", "ldw", p.cnt["ldw"])
            p.wait("sync", "ldg", p.cnt["ldg"])
        tpd = p.dmat("sync", xnT[:, i, :, :], xn_tm[:, i, :], "tp")
        p.wait("sync", "tp", tpd)
        tpd = p.dmat("sync", xqT[:, i, :, :], xq_tm[:, i, :], "tp")
        p.wait("sync", "tp", tpd)
    TP_DONE = p.cnt["tp"]
    P0_VV, P0_SS = p.cnt["vv"], p.cnt["ss"]
    p.wait("vector", "tp", TP_DONE)
    p.wait("vector", "ss", P0_SS)
    p.wait("scalar", "tp", TP_DONE)
    p.wait("scalar", "vv", P0_VV)
    p.wait("gpsimd", "tp", TP_DONE)

    if debug:
        stage0 = sb(es0, "dstage0", [128, NC * S // 4], F32)
        p.wait("vector", "tp", TP_DONE)
        for nm, src in [("xnT", xnT), ("xqT", xqT)]:
            for hf in range(4):
                half = NC * S // 4
                vs = p.op("vector", lambda v, src=src, hf=hf, half=half: v.tensor_copy(
                    stage0[:], src[:].rearrange("p n c t -> p (n c t)")[:, hf * half:(hf + 1) * half]), sem="vv")
                p.wait("sync", "vv", vs)
                sd = p.dma("sync", dbg[nm][:].rearrange("p c s -> p (c s)")[:, hf * half:(hf + 1) * half],
                           stage0[:], "st")
                p.wait("vector", "st", sd)
    es0.close()
    qT = sb(esP, "qT", [128, NC, S], BF16)
    kT = sb(esP, "kT", [128, NC, S], BF16)
    vt = sb(esP, "vt", [128, NT, H, 66], BF16)
    if upto < 1:
        return _finish(nc, p, [es5, esP])

    wbuf = sb(es1, "wbuf", [128, NC, 512], BF16)
    wmbuf = [sb(es1, f"wmbuf{i}", [128, 1024], F32) for i in range(2)]
    wsum = sb(es1, "wsum", [128, 32], F32)
    gred = sb(es1, "gred", [128, 1], F32)
    sqb = sb(es1, "sqb", [128, S], BF16)
    hsel = sb(es1, "hsel", [128, 2], BF16)
    normq = sb(es1, "normq", [2, NC], F32)
    normk = sb(es1, "normk", [2, NC], F32)
    tabs = sb(es1, "tabs", [16, NUM_BUCKETS], F32)
    tabm = sb(es1, "tabm", [16, 1], F32)
    bfars = sb(es1, "bfars", [1, 2 * H], F32)
    rowb = sb(es1, "rowb", [1, 128], F32)
    crow = sb(es1, "crow", [1, 3 * H], F32)

    pq = [es1.enter_context(nc.psum_tensor(f"pq{i}", [128, 512], F32)) for i in range(2)]
    p.wait("gpsimd", "tp", TP_DONE)
    ev = []
    vms = p.op("vector", lambda v: v.memset(vt[:].rearrange("p n h w -> p (n h w)"), 1.0), sem="vv")
    p.wait("scalar", "vv", vms)
    p.wait("tensor", "tp", TP_DONE)
    halves = []
    for w_h in (wqT_h, wkT_h, wvT_h):
        for hl in range(2):
            halves.append((w_h, hl))
    for hi, (w_h, hl) in enumerate(halves):
        p.wait("gpsimd", "tt", p.cnt["tt"])
        lw_ = p.dma("gpsimd", wbuf[:],
                    w_h[:, hl * 512:(hl + 1) * 512].rearrange("(c p) d -> p c d", p=128), "ldg")
        p.wait("tensor", "ldg", lw_)
        if hi < 4:
            dstT = qT if hi < 2 else kT
            scl = 0.125 if hi < 2 else 1.0
            for oc4 in range(4):
                oc = hl * 4 + oc4
                for qs in range(4):
                    pqt = pq[(oc4 * 4 + qs) % 2]
                    if len(ev) >= 2:
                        p.wait("tensor", "ss", ev[-2])
                    for ic in range(NC):
                        p.op("tensor", lambda t, ic=ic, oc=oc, oc4=oc4, qs=qs, pqt=pqt: t.matmul(
                            pqt[:], wbuf[:, ic, oc4 * 128:(oc4 + 1) * 128],
                            xnT[:, qs * 4:(qs + 1) * 4, ic, :],
                            start=(ic == 0), stop=(ic == NC - 1)),
                            sem="tt" if ic == NC - 1 else None)
                    tv = p.cnt["tt"]
                    p.wait("scalar", "tt", tv)
                    ev.append(p.op("scalar", lambda sc, dstT=dstT, oc=oc, qs=qs, pqt=pqt, scl=scl: sc.activation(
                        dstT[:, oc, qs * 512:(qs + 1) * 512], pqt[:], AF.Copy, scale=scl), sem="ss"))
        else:
            os_ = hl
            for tt_ in range(NT):
                pvt = pq[tt_ % 2]
                if len(ev) >= 2:
                    p.wait("tensor", "ss", ev[-2])
                for ic in range(NC):
                    p.op("tensor", lambda t, ic=ic, tt_=tt_, pvt=pvt: t.matmul(
                        pvt[:], xnT[:, tt_, ic, :],
                        wbuf[:, ic, :],
                        start=(ic == 0), stop=(ic == NC - 1)),
                        sem="tt" if ic == NC - 1 else None)
                tv = p.cnt["tt"]
                p.wait("scalar", "tt", tv)
                vo = vt[:, tt_, os_ * 8:(os_ + 1) * 8, 0:64]
                vi = pvt[:].rearrange("q (h e) -> q h e", e=64)
                ev.append(p.op("scalar", lambda sc, vo=vo, vi=vi: sc.activation(vo, vi, AF.Copy), sem="ss"))
    QKV_SS = p.cnt["ss"]

    wsums = sb(es1, "wsums", [128, 96], F32)
    onescol = sb(es1, "onescol", [128, 1], F32)
    p.op("vector", lambda v: v.memset(onescol[:], 1.0))
    mean_loads = []
    for wh, nrow, ncol, col in [(wgT_h, D, INNER, 0), (wuT_h, D, INNER, 1), (wdT_h, INNER, D, 2)]:
        for r in range(nrow // 128):
            for cc in range(0, ncol, 1024):
                mean_loads.append((wh, r, cc, min(1024, ncol - cc), col))
    gdone = []
    for i, (wh, r, cc, w_, col) in enumerate(mean_loads):
        bi = i % 2
        if i >= 2:
            p.wait("sync", "vv", gdone[i - 2])
        lms = "lm0" if bi == 0 else "lm1"
        lt = p.dma("sync", wmbuf[bi][:, 0:w_], wh[r * 128:(r + 1) * 128, cc:cc + w_], lms)
        p.wait("vector", lms, lt)
        p.opw("vector", lambda v, bi=bi, w_=w_: v.reduce_sum(
            wsum[:, 0:32], wmbuf[bi][:, 0:w_].rearrange("p (a b) -> p a b", b=32), AX,
            apply_absolute_value=True))
        gdone.append(p.op("vector", lambda v, i=i: v.reduce_sum(
            wsums[:, i:i + 1], wsum[:, 0:32], AX), sem="vv"))
    p.wait("vector", "vv", gdone[-1])
    colsum = wsum[:, 0:3]
    for mi in range(3):
        p.op("vector", lambda v, mi=mi: v.reduce_sum(
            colsum[:, mi:mi + 1], wsums[:, mi * 32:(mi + 1) * 32], AX),
            sem="vv" if mi == 2 else None)
    CS_VV = p.cnt["vv"]
    pnb = es1.enter_context(nc.psum_tensor("pnb", [128, 8], F32))
    pn0 = pnb
    p.wait("tensor", "vv", CS_VV)
    for mi in range(3):
        p.op("tensor", lambda t, mi=mi: t.matmul(
            pn0[0:1, mi:mi + 1], onescol[:], colsum[:, mi:mi + 1], start=True, stop=True),
            sem="tt" if mi == 2 else None)
    MS_TT = p.cnt["tt"]
    p.wait("vector", "tt", MS_TT)
    mstg = rowb[0:1, 96:104]
    p.op("vector", lambda v: v.memset(mstg, 0.0))
    p.op("vector", lambda v: v.tensor_scalar(mstg[:, 3:4], pn0[0:1, 0:1], 1.0 / (D * INNER), None, ALU.mult))
    p.op("vector", lambda v: v.tensor_scalar(mstg[:, 4:5], pn0[0:1, 1:2], 1.0 / (D * INNER), None, ALU.mult))
    p.opw("vector", lambda v: v.tensor_scalar(mstg[:, 6:7], pn0[0:1, 2:3], 1.0 / (D * INNER), None, ALU.mult))
    for c_ in (3, 4, 6):
        fn = p.opw if c_ == 6 else p.op
        fn("vector", lambda v, c_=c_: v.tensor_scalar(mstg[:, c_:c_ + 1], mstg[:, c_:c_ + 1], 1e-5, None, ALU.max))
    p.op("vector", lambda v: v.reciprocal(mstg[:, 0:1], mstg[:, 3:4]))
    p.op("vector", lambda v: v.reciprocal(mstg[:, 1:2], mstg[:, 4:5]))
    p.op("vector", lambda v: v.reciprocal(mstg[:, 2:3], mstg[:, 6:7]))
    p.opw("vector", lambda v: v.tensor_tensor(mstg[:, 5:6], mstg[:, 6:7], mstg[:, 4:5], ALU.mult))
    p.op("vector", lambda v: v.tensor_scalar(mstg[:, 5:6], mstg[:, 5:6], 0.25 / 127.0, None, ALU.mult))
    p.op("vector", lambda v: v.tensor_scalar(mstg[:, 3:4], mstg[:, 3:4], 0.5, None, ALU.mult))
    MCV = p.op("vector", lambda v: v.tensor_scalar(mstg[:, 7:8], mstg[:, 7:8], 0.0, None, ALU.mult), sem="vv")
    p.wait("tensor", "vv", MCV)
    mmb = p.op("tensor", lambda t: t.matmul(pnb[:], onesr[:], mstg, start=True, stop=True), sem="tt")
    p.wait("vector", "tt", mmb)
    MC = p.op("vector", lambda v: v.tensor_copy(mcol[:], pnb[:]), sem="vv")

    pn = es1.enter_context(nc.psum_tensor("pn", [2, 2048], F32))
    p.op("vector", lambda v: v.memset(hsel[:, :], 0.0))
    p.op("vector", lambda v: v.memset(hsel[0:64, 0:1], 1.0))
    vsel = p.op("vector", lambda v: v.memset(hsel[64:128, 1:2], 1.0), sem="vv")
    p.wait("tensor", "vv", vsel)
    p.wait("tensor", "ss", QKV_SS)
    p.wait("vector", "ss", QKV_SS)
    for srcT, dstn in [(qT, normq), (kT, normk)]:
        for c in range(NC):
            vq = p.op("vector", lambda v, srcT=srcT, c=c: v.tensor_tensor(
                sqb[:], srcT[:, c, :], srcT[:, c, :], ALU.mult), sem="vv")
            p.wait("tensor", "vv", vq)
            for qs in range(4):
                p.op("tensor", lambda t, qs=qs: t.matmul(
                    pn[:, qs * 512:(qs + 1) * 512], hsel[:], sqb[:, qs * 512:(qs + 1) * 512],
                    start=True, stop=True), sem="tt" if qs == 3 else None)
            tv = p.cnt["tt"]
            p.wait("vector", "tt", tv)
            p.op("vector", lambda v, dstn=dstn, c=c: v.reduce_max(dstn[:, c:c + 1], pn[:], AX), sem="vv")
    NORM_VV = p.cnt["vv"]
    lt = p.dma("sync", tabs[:], tabT_h[:], "ldw")
    p.wait("vector", "ldw", lt)
    vm = p.op("vector", lambda v: v.reduce_max(tabm[:], tabs[:], AX), sem="vv")
    p.wait("sync", "vv", NORM_VV)
    p.wait("sync", "vv", vm)
    p.dma("sync", nrm_scr[0:2, 0, :], normq[:], "tp")
    p.dma("sync", nrm_scr[0:2, 1, :], normk[:], "tp")
    p.dma("sync", tab_scr[:], tabm[:], "tp")
    p.wait("sync", "tp", p.cnt["tp"])
    p.dma("sync", rowb[0:1, 0:16], nrm_scr[:, 0, :].rearrange("q c -> c q").unsqueeze(0), "tp",
          allow_slow_non_contiguous=True)
    p.dma("sync", rowb[0:1, 16:32], nrm_scr[:, 1, :].rearrange("q c -> c q").unsqueeze(0), "tp",
          allow_slow_non_contiguous=True)
    p.dma("sync", rowb[0:1, 32:48], tab_scr[:].rearrange("h o -> o h"), "tp",
          allow_slow_non_contiguous=True)
    lb = p.dma("sync", bfars[:], bfar_h[:], "ldw")
    p.wait("scalar", "tp", p.cnt["tp"])
    sc1 = p.op("scalar", lambda s_: s_.activation(rowb[0:1, 48:64], rowb[0:1, 0:16], AF.Sqrt), sem="ss")
    sc2 = p.op("scalar", lambda s_: s_.activation(rowb[0:1, 64:80], rowb[0:1, 16:32], AF.Sqrt), sem="ss")
    p.wait("vector", "ss", sc2)
    p.wait("vector", "ldw", lb)
    p.opw("vector", lambda v: v.tensor_tensor(rowb[0:1, 48:64], rowb[0:1, 48:64], rowb[0:1, 64:80], ALU.mult))
    p.opw("vector", lambda v: v.tensor_tensor(rowb[0:1, 48:64], rowb[0:1, 48:64], rowb[0:1, 32:48], ALU.add))
    p.op("vector", lambda v: v.tensor_tensor(crow[0:1, 0:16], bfars[0:1, 0:16], rowb[0:1, 48:64], ALU.subtract))
    p.op("vector", lambda v: v.tensor_tensor(crow[0:1, 16:32], bfars[0:1, 16:32], rowb[0:1, 48:64], ALU.subtract))
    cv = p.op("vector", lambda v: v.tensor_scalar(crow[0:1, 32:48], rowb[0:1, 48:64], -1.0, None, ALU.mult), sem="vv")
    p.wait("tensor", "vv", cv)
    pnc = es1.enter_context(nc.psum_tensor("pnc", [128, 3 * H], F32))
    gcb = p.op("tensor", lambda t: t.matmul(pnc[:], onesr[:], crow[0:1, :], start=True, stop=True), sem="tt")
    p.wait("vector", "tt", gcb)
    CBC_GG = p.op("vector", lambda v: v.tensor_copy(cbc[:], pnc[:]), sem="vv")

    if debug >= 2:
        p.wait("sync", "ss", p.cnt["ss"])
        stage1 = sb(es1, "dstage1", [128, NC * S // 4], F32)
        for nm, src in [("qT", qT), ("kT", kT)]:
            for hf in range(4):
                half = NC * S // 4
                vs = p.op("vector", lambda v, src=src, hf=hf, half=half: v.tensor_copy(
                    stage1[:], src[:].rearrange("p c s -> p (c s)")[:, hf * half:(hf + 1) * half]), sem="vv")
                p.wait("sync", "vv", vs)
                sd = p.dma("sync", dbg[nm][:].rearrange("p c s -> p (c s)")[:, hf * half:(hf + 1) * half],
                           stage1[:], "st")
                p.wait("vector", "st", sd)
        for hf in range(6):
            hw_ = NT * H * 66 // 6
            vs = p.op("vector", lambda v, hf=hf, hw_=hw_: v.tensor_copy(
                stage1[:, 0:hw_], vt[:].rearrange("p n h w -> p (n h w)")[:, hf * hw_:(hf + 1) * hw_]), sem="vv")
            p.wait("sync", "vv", vs)
            sd = p.dma("sync", dbg["vt"][:].rearrange("p n h w -> p (n h w)")[:, hf * hw_:(hf + 1) * hw_],
                       stage1[:, 0:hw_], "st")
            p.wait("vector", "st", sd)
        p.wait("gpsimd", "vv", CBC_GG)
        p.dma("gpsimd", dbg["cbc"][:], cbc[:], "gst")
        p.wait("gpsimd", "vv", MC)
        p.dma("gpsimd", dbg["mcol"][:], mcol[:], "gst")
    es1.close()
    if upto < 2:
        return _finish(nc, p, [es5, esP])

    rsxb = sb(es3, "rsxb", [128, S], F32)
    macc = sb(es3, "macc", [128, S], F32)
    wjit = [sb(es2, f"wjit{i}", [128, NC, 128], F32) for i in range(2)]
    wdjit = sb(es2, "wdjit", [128, 1024], F32)
    wq16 = sb(es2, "wq16", [128, 1024], F16)
    wqs = sb(es2, "wqs", [128, 1024], F32)
    wqg = [[sb(es2, f"wqg{par}{gi}", [128, NC, 128], BF16) for gi in range(2)] for par in range(2)]
    asb = sb(es2, "asb", [128, 1024], F32)
    wdqb = sb(es2, "wdqb", [128, 1024], BF16)
    halfc = sb(es2, "halfc", [128, 2], F32)
    gbuf = sb(es2, "gbuf", [128, 1024], F32)
    silb = sb(es2, "silb", [128, 1024], F32)
    habs = sb(es2, "habs", [128, 1024], F32)
    hbuf = [sb(es2, f"hbuf{i}", [128, 1024], F32) for i in range(2)]
    pg = [es2.enter_context(nc.psum_tensor(f"pg{i}", [128, 1024], F32)) for i in range(2)]
    pu = [es2.enter_context(nc.psum_tensor(f"pu{i}", [128, 1024], F32)) for i in range(2)]

    p.wait("sync", "tt", p.cnt["tt"])
    p.wait("sync", "ss", p.cnt["ss"])
    p.wait("sync", "vv", p.cnt["vv"])
    rg = p.dma("sync", sxr_scr[:], sxr_tm[:], "tp")
    p.wait("sync", "tp", rg)
    rg = p.dma("sync", macc[0:1, :], sxr_scr.rearrange("p n -> n p").unsqueeze(0), "tp",
               allow_slow_non_contiguous=True)
    p.wait("tensor", "ss", QKV_SS)
    p.wait("tensor", "vv", NORM_VV)
    p.wait("tensor", "tp", p.cnt["tp"])
    for qs in range(4):
        p.op("tensor", lambda t, qs=qs: t.matmul(
            pg[qs // 2][:, (qs % 2) * 512:(qs % 2 + 1) * 512], onesr[:],
            macc[0:1, qs * 512:(qs + 1) * 512], start=True, stop=True),
            sem="tt" if qs == 3 else None)
    grb = p.cnt["tt"]
    p.wait("vector", "tt", grb)
    for qs in range(4):
        p.op("vector", lambda v, qs=qs: v.tensor_copy(
            rsxb[:, qs * 512:(qs + 1) * 512], pg[qs // 2][:, (qs % 2) * 512:(qs % 2 + 1) * 512]))
    MACC0 = p.op("vector", lambda v: v.memset(macc[:], 0.0), sem="vv")

    if debug:
        p.wait("sync", "vv", MACC0)
        p.dma("sync", dbg["rsxb"][:], rsxb[:], "st")

    p.op("vector", lambda v: v.memset(halfc[:, 0:1], -0.5))
    HC = p.op("vector", lambda v: v.memset(halfc[:, 1:2], 0.5), sem="vv")
    p.wait("scalar", "vv", HC)
    p.wait("tensor", "ss", QKV_SS)
    p.wait("tensor", "vv", NORM_VV)
    p.wait("vector", "vv", MC)
    p.wait("scalar", "vv", MC)

    wgT_cols = wgT_h.rearrange("(c p) (b j) -> b p c j", p=128, j=128)
    wuT_cols = wuT_h.rearrange("(c p) (b j) -> b p c j", p=128, j=128)
    wdT_rows = wdT_h.rearrange("(b j) o -> b j o", j=128)
    wdq_rows = wdq_scr.rearrange("(b j) o -> b j o", j=128)
    h_rows = h_scr.rearrange("(b j) s -> b j s", j=128)

    ldq = {}
    hv_hist = []
    dh_hist = []
    wd_vv = []
    wd_st = []
    mm_tt = {}

    def quant_block(ib):
        if ib >= 2:
            p.wait("vector", "tt", mm_tt[ib - 2])
        for gi, wcols in enumerate([wgT_cols, wuT_cols]):
            buf = wjit[gi]
            if (ib - 1, gi) in ldq:
                p.wait("sync", "vv", ldq[(ib - 1, gi)])
            lane = "lw0" if gi == 0 else "lw1"
            lt = p.dma("sync", buf[:], wcols[ib], lane)
            p.wait("scalar", lane, lt)
            rcol = mcol[:, 0:1] if gi == 0 else mcol[:, 1:2]
            p.wait("scalar", "vv", p.cnt["vv"])
            p.op("scalar", lambda sc, buf=buf, rcol=rcol: sc.activation(
                wqs[:], buf[:].rearrange("p c j -> p (c j)"), AF.Sign, bias=halfc[:, 0:1], scale=rcol))
            s2 = p.op("scalar", lambda sc, buf=buf, rcol=rcol: sc.activation(
                habs[:], buf[:].rearrange("p c j -> p (c j)"), AF.Sign, bias=halfc[:, 1:2], scale=rcol), sem="ss")
            p.wait("vector", "ss", s2)
            ldq[(ib, gi)] = p.op("vector", lambda v, gi=gi, ib=ib: v.tensor_tensor(
                wqg[ib % 2][gi][:].rearrange("p c j -> p (c j)"), wqs[:], habs[:], ALU.add), sem="vv")
        if wd_vv:
            p.wait("sync", "vv", wd_vv[-1])
        lt = p.dma("sync", wdjit[:], wdT_rows[ib], "lw2")
        p.wait("scalar", "lw2", lt)
        if wd_st:
            p.wait("vector", "fs0", wd_st[-1])
        p.wait("scalar", "vv", p.cnt["vv"])
        p.op("scalar", lambda sc: sc.activation(wqs[:], wdjit[:], AF.Sign, bias=halfc[:, 0:1], scale=mcol[:, 2:3]))
        s2 = p.op("scalar", lambda sc: sc.activation(habs[:], wdjit[:], AF.Sign, bias=halfc[:, 1:2], scale=mcol[:, 2:3]), sem="ss")
        p.wait("vector", "ss", s2)
        vwd = p.op("vector", lambda v: v.tensor_tensor(wdqb[:], wqs[:], habs[:], ALU.add), sem="vv")
        wd_vv.append(vwd)
        p.wait("sync", "vv", vwd)
        wd_st.append(p.dma("sync", wdq_rows[ib], wdqb[:], "fs0"))

    quant_block(0)
    for ib in range(NJ):
        if ib + 1 < NJ:
            quant_block(ib + 1)
        for th in range(2):
            pgt, put = pg[th], pu[th]
            p.wait("tensor", "vv", ldq[(ib, 1)])
            if len(hv_hist) >= 2:
                p.wait("tensor", "vv", hv_hist[-2])
            for gi, (ppp, wsrc) in enumerate([(pgt, wqg[ib % 2][0]), (put, wqg[ib % 2][1])]):
                for ic in range(NC):
                    for hf in range(2):
                        p.op("tensor", lambda t, ppp=ppp, wsrc=wsrc, ic=ic, th=th, hf=hf: t.matmul(
                            ppp[:, hf * 512:(hf + 1) * 512],
                            wsrc[:, ic, :],
                            xqT[:, th * 8 + hf * 4:th * 8 + (hf + 1) * 4, ic, :],
                            start=(ic == 0), stop=(ic == NC - 1)),
                            sem="tt" if (ic == NC - 1 and hf == 1 and gi == 1) else None)
            tv = p.cnt["tt"]
            mm_tt[ib] = tv
            p.wait("vector", "tt", tv)
            p.wait("vector", "ss", p.cnt["ss"])
            vg = p.op("vector", lambda v, pgt=pgt, th=th: v.scalar_tensor_tensor(
                gbuf[:], pgt[:], mcol[:, 3:4], rsxb[:, th * 1024:(th + 1) * 1024],
                ALU.mult, ALU.mult), sem="vv")
            p.wait("scalar", "vv", vg)
            ssil = p.op("scalar", lambda sc: sc.activation(silb[:], gbuf[:], AF.Silu), sem="ss")
            p.wait("vector", "ss", ssil)
            hb = hbuf[(ib * 2 + th) % 2]
            hlane = "dh0" if (ib * 2 + th) % 2 == 0 else "dh1"
            if len(dh_hist) >= 2:
                p.wait("vector", hlane, dh_hist[-2])
            vh = p.op("vector", lambda v, put=put, hb=hb: v.tensor_tensor(hb[:], silb[:], put[:], ALU.mult), sem="vv")
            hv_hist.append(vh)
            p.wait("scalar", "vv", vh)
            sab = p.op("scalar", lambda sc, hb=hb: sc.activation(asb[:], hb[:], AF.Abs), sem="ss")
            p.wait("vector", "ss", sab)
            p.op("vector", lambda v, th=th: v.tensor_max(
                macc[:, th * 1024:(th + 1) * 1024], macc[:, th * 1024:(th + 1) * 1024], asb[:]))
            p.wait("sync", "vv", vh)
            dh_hist.append(p.dma("sync", h_rows[ib][:, th * 1024:(th + 1) * 1024], hb[:], hlane))
    P2_DH0, P2_DH1 = p.cnt["dh0"], p.cnt["dh1"]

    sbcb, ebcb = macc, rsxb
    shsem = p.cnt["vv"]
    for half in range(2):
        cs_, ce_ = half * 1024, (half + 1) * 1024
        n = 64
        while n >= 1:
            p.wait("sync", "vv", p.cnt["vv"])
            sh = p.dma("sync", hbuf[half][0:n, 0:1024] if n > 0 else None,
                       macc[n:2 * n, cs_:ce_], "tp")
            p.wait("vector", "tp", sh)
            p.op("vector", lambda v, n=n, cs_=cs_, ce_=ce_, half=half: v.tensor_max(
                macc[0:n, cs_:ce_], macc[0:n, cs_:ce_], hbuf[half][0:n, 0:1024]),
                sem="vv")
            n //= 2
    MXV = p.cnt["vv"]
    p.wait("tensor", "vv", MXV)
    for qs in range(4):
        p.op("tensor", lambda t, qs=qs: t.matmul(
            pg[qs // 2][:, (qs % 2) * 512:(qs % 2 + 1) * 512], onesr[:],
            macc[0:1, qs * 512:(qs + 1) * 512], start=True, stop=True),
            sem="tt" if qs == 3 else None)
    gpr = p.cnt["tt"]
    p.wait("vector", "tt", gpr)
    for qs in range(4):
        p.op("vector", lambda v, qs=qs: v.tensor_copy(
            macc[:, qs * 512:(qs + 1) * 512], pg[qs // 2][:, (qs % 2) * 512:(qs % 2 + 1) * 512]))
    p.op("vector", lambda v: v.tensor_scalar(macc[:], macc[:], 1e-12, None, ALU.max))
    p.op("vector", lambda v: v.tensor_tensor(rsxb[:], rsxb[:], macc[:], ALU.mult))
    veb = p.op("vector", lambda v: v.tensor_scalar(rsxb[:], rsxb[:], mcol[:, 5:6], None, ALU.mult), sem="vv")
    p.op("vector", lambda v: v.reciprocal(hbuf[0][:], macc[:, 0:1024]))
    p.op("vector", lambda v: v.reciprocal(hbuf[1][:], macc[:, 1024:2048]))
    p.op("vector", lambda v: v.tensor_scalar(macc[:, 0:1024], hbuf[0][:], 127.0, None, ALU.mult))
    veb = p.op("vector", lambda v: v.tensor_scalar(macc[:, 1024:2048], hbuf[1][:], 127.0, None, ALU.mult), sem="vv")

    if debug >= 3:
        xq2stage = sb(es2, "xq2stage", [128, 4096], F32)
        p.wait("vector", "tt", p.cnt["tt"])
        for hf in range(4):
            half = NC * S // 4
            vs = p.op("vector", lambda v, hf=hf, half=half: v.tensor_copy(
                xq2stage[:], xqT[:].rearrange("p n c t -> p (n c t)")[:, hf * half:(hf + 1) * half]), sem="vv")
            p.wait("sync", "vv", vs)
            sd = p.dma("sync", dbg["xq2"][:].rearrange("p c s -> p (c s)")[:, hf * half:(hf + 1) * half],
                       xq2stage[:], "st")
            p.wait("vector", "st", sd)
        p.wait("sync", "dh0", P2_DH0)
        p.wait("sync", "dh1", P2_DH1)
        p.wait("sync", "vv", p.cnt["vv"])
        hsv = dbg["hscr"].rearrange("(b j) s -> b j s", j=128)
        hstage = xq2stage
        for jb in range(NJ):
            ldh = p.dma("sync", hstage[:, 0:2048], h_rows[jb], "ldw")
            p.wait("sync", "ldw", ldh)
            p.dma("sync", hsv[jb], hstage[:, 0:2048], "st")
        p.dma("sync", dbg["sbc"][:], sbcb[:], "st")
        p.dma("sync", dbg["ebc"][:], ebcb[:], "st")
        p.wait("sync", "st", p.cnt["st"])
    es2.close()
    if upto < 3:
        return _finish(nc, p, [es5, esP])

    fstage = [sb(es3, f"fstage{i}", [128, 1024], F32) for i in range(2)]
    hjit = [sb(es3, f"hjit{i}", [128, 512], F32) for i in range(3)]
    hi32 = sb(es3, "hi32", [128, 512], I32)
    hrb = [sb(es3, f"hrb{i}", [128, 512], BF16) for i in range(2)]
    hc1 = sb(es3, "hc1", [128, 512], BF16)
    hqm = [sb(es3, f"hqm{i}", [128, 512], F32) for i in range(2)]
    hqb = [sb(es3, f"hqb{i}", [128, 512], BF16) for i in range(2)]
    wdl = [sb(es3, f"wdl{i}", [128, 1024], BF16) for i in range(3)]
    pd = [es3.enter_context(nc.psum_tensor(f"pd{i}", [128, 512], F32)) for i in range(8)]
    ffn_rows = ffn_scr.rearrange("(c o) s -> c o s", o=128)

    p.wait("scalar", "dh0", P2_DH0)
    p.wait("scalar", "dh1", P2_DH1)
    p.wait("scalar", "vv", p.cnt["vv"])
    p.wait("sync", "fs0", p.cnt["fs0"])
    p.wait("sync", "tt", p.cnt["tt"])
    p.wait("sync", "ss", p.cnt["ss"])
    p.wait("sync", "vv", p.cnt["vv"])
    p.wait("vector", "vv", veb)
    p.wait("tensor", "vv", veb)
    dn_at = {}
    vq_at = {}
    lw_at = {}
    sh2_at = {}
    yv_at = {}
    fd_hist = []
    vf_last = 0
    NIT = 4 * NJ

    def qa(it):
        tq, jc = divmod(it, NJ)
        pr, r3 = it % 2, it % 3
        if it >= 3:
            p.wait("scalar", "vv", vq_at[it - 3])
            p.wait("sync", "dn", dn_at[it - 3])
        hlane = "dh0" if pr == 0 else "dh1"
        lh = p.dma("scalar", hjit[r3][:], h_rows[jc][:, tq * 512:(tq + 1) * 512], hlane)
        lw_at[it] = p.dma("sync", wdl[r3][:], wdq_rows[jc], "lw0" if pr == 0 else "lw1")
        p.wait("vector", hlane, lh)
        if it >= 2:
            p.wait("vector", "ss", sh2_at[it - 2])
        yv_at[it] = p.op("vector", lambda v, r3=r3, pr=pr, tq=tq: v.tensor_tensor(
            hqm[pr][:], hjit[r3][:], sbcb[:, tq * 512:(tq + 1) * 512], ALU.mult), sem="vv")
        p.wait("scalar", "vv", yv_at[it])
        p.op("scalar", lambda sc, pr=pr: sc.activation(hi32[:], hqm[pr][:], AF.Copy, bias=-1.0, scale=1.0))
        sh2_at[it] = p.op("scalar", lambda sc, pr=pr: sc.activation(
            hrb[pr][:], hi32[:], AF.Copy, bias=0.0, scale=1.0), sem="ss")

    def qb(it):
        pr = it % 2
        p.wait("vector", "ss", sh2_at[it])
        if it >= 2:
            p.wait("vector", "dn", dn_at[it - 2])
        p.op("vector", lambda v, pr=pr: v.tensor_tensor(hqm[pr][:], hqm[pr][:], hrb[pr][:], ALU.subtract))
        p.op("vector", lambda v, pr=pr: v.scalar_tensor_tensor(
            hc1[:], hqm[pr][:], 0.5, hrb[pr][:], ALU.is_ge, ALU.add))
        vq_at[it] = p.op("vector", lambda v, pr=pr: v.scalar_tensor_tensor(
            hqb[pr][:], hqm[pr][:], 1.5, hc1[:], ALU.is_ge, ALU.add), sem="vv")

    qa(0)
    qa(1)
    qb(0)
    for tq in range(4):
        for jc in range(NJ):
            it = tq * NJ + jc
            bi = it % 2
            if it + 2 < NIT:
                qa(it + 2)
            if it + 1 < NIT:
                qb(it + 1)
            p.wait("tensor", "vv", vq_at[it])
            p.wait("tensor", "lw0" if bi == 0 else "lw1", lw_at[it])
            if jc == 0 and tq > 0:
                p.wait("tensor", "vv", vf_last)
            for ob in range(8):
                p.op("tensor", lambda t, ob=ob, jc=jc, it=it: t.matmul(
                    pd[ob][:], wdl[it % 3][:].rearrange("j (a o) -> j a o", o=128)[:, ob, :],
                    hqb[it % 2][:],
                    start=(jc == 0), stop=(jc == NJ - 1)),
                    sem="dn" if ob == 7 else None)
            dn_at[it] = p.cnt["dn"]
        p.wait("vector", "dn", p.cnt["dn"])
        for ob in range(8):
            fs = fstage[ob % 2]
            flane = "fs0" if ob % 2 == 0 else "fs1"
            if len(fd_hist) >= 2:
                p.wait("vector", fd_hist[-2][1], fd_hist[-2][0])
            vf = p.op("vector", lambda v, ob=ob, tq=tq, fs=fs: v.tensor_tensor(
                fs[:, 0:512], pd[ob][:], ebcb[:, tq * 512:(tq + 1) * 512], ALU.mult), sem="vv")
            vf_last = vf
            p.wait("sync", "vv", vf)
            fd_hist.append((p.dma("sync", ffn_rows[ob][:, tq * 512:(tq + 1) * 512], fs[:, 0:512], flane), flane))

    if debug >= 4:
        p.wait("sync", "fs0", p.cnt["fs0"])
        p.wait("sync", "fs1", p.cnt["fs1"])
        fsv = dbg["ffn"].rearrange("(c o) s -> c o s", o=128)
        fstg = sb(es3, "fstg", [128, 2048], F32)
        for c in range(NC):
            ldf = p.dma("sync", fstg[:], ffn_rows[c], "ldw")
            p.wait("sync", "ldw", ldf)
            p.dma("sync", fsv[c], fstg[:], "st")
        p.wait("sync", "st", p.cnt["st"])
    es3.close()
    if upto < 4:
        return _finish(nc, p, [es5, esP])

    es4a = ExitStack()
    attnT = sb(es4, "attnT", [128, NC, S], BF16)
    bv2t = sb(es4a, "bv2t_s", [128, H, BW], BF16)
    ring = sb(es4a, "ring", [128, 3, 1024], BF16)
    att_u = sb(es4a, "att_u", [64, 1024], BF16)
    denst = sb(es4a, "denst", [128, 1024], F32)
    ps_s = [es4a.enter_context(nc.psum_tensor(f"ps_s{i}", [128, 1024], F32)) for i in range(2)]
    pav = [es4a.enter_context(nc.psum_tensor(f"pav{i}", [65, 1024], F32)) for i in range(2)]

    p.wait("gpsimd", "st", p.cnt["st"])
    p.wait("gpsimd", "fs0", p.cnt["fs0"])
    p.wait("gpsimd", "fs1", p.cnt["fs1"])
    p.wait("gpsimd", "dn", p.cnt["dn"])
    p.wait("gpsimd", "vv", p.cnt["vv"])
    lbv = p.dma("gpsimd", bv2t[:],
                bv2t_h.rearrange("h p w -> p h w"), "ldg")
    p.wait("tensor", "ldg", lbv)
    p.wait("tensor", "vv", CBC_GG)
    p.wait("tensor", "vv", vf_last)
    p.wait("scalar", "vv", CBC_GG)

    exp_at = {}
    avkey = {}
    evdone = {}
    den_dma = []
    odd_dma = []

    def _av_step(h, qm, j, pavt):
        p.wait("tensor", "ex", exp_at[(h, qm, j)])
        for qh in range(2):
            p.op("tensor", lambda t, h=h, j=j, qh=qh, pavt=pavt: t.matmul(
                pavt[:, qh * 512:(qh + 1) * 512],
                vt[:, j, h, 0:65],
                ring[:, j % 3, qh * 512:(qh + 1) * 512],
                start=(j == 0), stop=(j == NT - 1)),
                sem="av" if qh == 1 else None)
        avkey[(h, qm, j)] = p.cnt["av"]
    for h in range(H):
        for qm in range(2):
            pavt = pav[qm]
            if h >= 1:
                prevev = evdone[(h - 1, qm)]
                if isinstance(prevev, tuple):
                    p.wait("tensor", prevev[0], prevev[1])
                else:
                    p.wait("tensor", "dv", prevev)
            base = (h % 2) * 64
            for kc in range(NT):
                pst = ps_s[kc % 2]
                if kc >= 2:
                    p.wait("tensor", "ex", exp_at[(h, qm, kc - 2)])
                elif (h, qm) != (0, 0):
                    pk = (h, qm - 1) if qm == 1 else (h - 1, 1)
                    p.wait("tensor", "ex", exp_at[(pk[0], pk[1], NT - 2 + kc)])
                kinds = []
                for qh in range(2):
                    q0 = qm * 1024 + qh * 512
                    kind, arg = _near(kc, q0 // 512)
                    kinds.append((kind, arg))
                    p.op("tensor", lambda t, kc=kc, q0=q0, qh=qh, pst=pst, base=base, h=h,
                         stop=(kind == "far"): t.matmul(
                        pst[:, qh * 512:(qh + 1) * 512],
                        kT[base:base + 64, h // 2, kc * 128:(kc + 1) * 128],
                        qT[base:base + 64, h // 2, q0:q0 + 512],
                        start=True, stop=stop),
                        sem="qk" if (qh == 1 and kind == "far") else None)
                    if kind == "near":
                        p.op("tensor", lambda t, h=h, arg=arg, qh=qh, pst=pst: t.matmul(
                            pst[:, qh * 512:(qh + 1) * 512], idmb[:],
                            bv2t[:, h, arg:arg + 512],
                            start=False, stop=True),
                            sem="qk" if qh == 1 else None)
                qkv_ = p.cnt["qk"]
                p.wait("scalar", "qk", qkv_)
                slot = kc % 3
                if kc >= 3:
                    p.wait("scalar", "av", avkey[(h, qm, kc - 3)])
                elif (h, qm) != (0, 0):
                    pk = (h, qm - 1) if qm == 1 else (h - 1, 1)
                    p.wait("scalar", "av", avkey[(pk[0], pk[1], NT - 3 + kc)])
                (k0, a0), (k1, a1) = kinds
                c0 = (a0 if k0 == "far" else 2) * H + h
                c1 = (a1 if k1 == "far" else 2) * H + h
                if c0 == c1:
                    exp_at[(h, qm, kc)] = p.op("scalar", lambda sc, pst=pst, slot=slot, c0=c0: sc.activation(
                        ring[:, slot, :], pst[:], AF.Exp, bias=cbc[:, c0:c0 + 1], scale=1.0), sem="ex")
                else:
                    for qh, cc in [(0, c0), (1, c1)]:
                        exp_at[(h, qm, kc)] = p.op("scalar", lambda sc, pst=pst, qh=qh, slot=slot, cc=cc: sc.activation(
                            ring[:, slot, qh * 512:(qh + 1) * 512], pst[:, qh * 512:(qh + 1) * 512],
                            AF.Exp, bias=cbc[:, cc:cc + 1], scale=1.0), sem="ex")
                if kc >= 1:
                    _av_step(h, qm, kc - 1, pavt)
            _av_step(h, qm, NT - 1, pavt)
            p.wait("vector", "av", avkey[(h, qm, NT - 1)])
            if den_dma:
                p.wait("vector", "dd", den_dma[-1])
            dvc = p.op("vector", lambda v, pavt=pavt: v.tensor_copy(denst[64:65, :], pavt[64:65, :]), sem="dv")
            p.wait("sync", "dv", dvc)
            den_dma.append(p.dma("sync", den_scr[0:1, (h * 2 + qm) * 1024:(h * 2 + qm + 1) * 1024],
                                 denst[64:65, :], "dd"))
            if h % 2 == 0:
                ev_ = p.op("vector", lambda v, h=h, qm=qm, pavt=pavt: v.tensor_copy(
                    attnT[0:64, h // 2, qm * 1024:(qm + 1) * 1024], pavt[0:64, :]), sem="dv")
            else:
                if odd_dma:
                    p.wait("vector", "do", odd_dma[-1])
                ev0 = p.op("vector", lambda v, pavt=pavt: v.tensor_copy(att_u[:, :], pavt[0:64, :]), sem="dv")
                p.wait("gpsimd", "dv", ev0)
                odd_dma.append(p.dma("gpsimd", attnT[64:128, h // 2, qm * 1024:(qm + 1) * 1024], att_u[:, :], "do"))
                ev_ = ("do", odd_dma[-1])
            evdone[(h, qm)] = ev_
    P4_DV = p.cnt["dv"]

    es4a.close()
    dens2 = sb(es4, "dens2", [128, 256], F32)
    rcpr = [sb(es4, f"rcpr{i}", [1, S], F32) for i in range(2)]
    pdv = es4.enter_context(nc.psum_tensor("pdv", [128, S], F32))
    p.wait("sync", "dv", P4_DV)
    p.wait("sync", "dd", p.cnt["dd"])
    p.wait("sync", "do", p.cnt["do"])
    p.wait("tensor", "dv", P4_DV)
    p.wait("tensor", "dd", p.cnt["dd"])
    p.wait("tensor", "do", p.cnt["do"])
    ld_d = p.dma("sync", dens2[:], den_scr[0:1, :].rearrange("o (q c) -> (o q) c", c=256), "ldw")
    p.wait("vector", "ldw", ld_d)
    vr = p.op("vector", lambda v: v.reciprocal(dens2[:], dens2[:]), sem="vv")
    p.wait("sync", "vv", vr)
    wr_d = p.dma("sync", rcp_scr[0:1, :].rearrange("o (q c) -> (o q) c", c=256), dens2[:], "dv")
    p.wait("sync", "dv", p.cnt["dv"])
    DIV_VV = 0
    for c in range(NC):
        lv0 = p.dma("sync", rcpr[0][:], rcp_scr[0:1, (2 * c) * S:(2 * c + 1) * S], "ldw")
        lv = p.dma("sync", rcpr[1][:], rcp_scr[0:1, (2 * c + 1) * S:(2 * c + 2) * S], "ldw")
        p.wait("tensor", "ldw", lv)
        if DIV_VV:
            p.wait("tensor", "vv", DIV_VV)
        for qs in range(4):
            p.op("tensor", lambda t, qs=qs: t.matmul(
                pdv[0:64, qs * 512:(qs + 1) * 512], onesr[0:1, 0:64],
                rcpr[0][0:1, qs * 512:(qs + 1) * 512], start=True, stop=True))
            p.op("tensor", lambda t, qs=qs: t.matmul(
                pdv[64:128, qs * 512:(qs + 1) * 512], onesr[0:1, 0:64],
                rcpr[1][0:1, qs * 512:(qs + 1) * 512], start=True, stop=True,
                tile_position=(0, 64)),
                sem="tt" if qs == 3 else None)
        dvt = p.cnt["tt"]
        p.wait("vector", "tt", dvt)
        p.op("vector", lambda v, c=c: v.tensor_tensor(
            attnT[:, c, 0:1024], attnT[:, c, 0:1024], pdv[:, 0:1024], ALU.mult))
        DIV_VV = p.op("vector", lambda v, c=c: v.tensor_tensor(
            attnT[:, c, 1024:2048], attnT[:, c, 1024:2048], pdv[:, 1024:2048], ALU.mult), sem="vv")
        p.wait("sync", "vv", DIV_VV)

    if debug >= 5:
        stg4 = sb(es4, "dstage4", [128, NC * S // 4], F32)
        for hf in range(4):
            half = NC * S // 4
            vs = p.op("vector", lambda v, hf=hf, half=half: v.tensor_copy(
                stg4[:], attnT[:].rearrange("p c s -> p (c s)")[:, hf * half:(hf + 1) * half]), sem="vv")
            p.wait("sync", "vv", vs)
            sd = p.dma("sync", dbg["attnT"][:].rearrange("p c s -> p (c s)")[:, hf * half:(hf + 1) * half],
                       stg4[:], "st")
            p.wait("vector", "st", sd)

    wob = sb(es4, "wob", [128, NC, D], BF16)
    fstage = [sb(es4, f"fstage5{i}", [128, 1024], F32) for i in range(2)]
    p.wait("gpsimd", "ex", p.cnt["ex"])
    p.wait("gpsimd", "av", p.cnt["av"])
    p.wait("gpsimd", "vv", p.cnt["vv"])
    lwo = p.dma("gpsimd", wob[:],
                woT_h.rearrange("(c p) d -> p c d", p=128), "ldg")
    pw = es4.enter_context(nc.psum_tensor("pw", [128, 1024], F32))
    att_rows = att_scr.rearrange("(c o) s -> c o s", o=128)
    p.wait("tensor", "ldg", lwo)
    p.wait("tensor", "vv", DIV_VV)
    vw_hist = []
    aw_hist = []
    for oc in range(NC):
        for th in range(2):
            if vw_hist:
                p.wait("tensor", "vv", vw_hist[-1])
            for ic in range(NC):
                for hf in range(2):
                    p.op("tensor", lambda t, oc=oc, ic=ic, th=th, hf=hf: t.matmul(
                        pw[:, hf * 512:(hf + 1) * 512],
                        wob[:, ic, oc * 128:(oc + 1) * 128],
                        attnT[:, ic, th * 1024 + hf * 512:th * 1024 + (hf + 1) * 512],
                        start=(ic == 0), stop=(ic == NC - 1)),
                        sem="tt" if (ic == NC - 1 and hf == 1) else None)
            tv = p.cnt["tt"]
            p.wait("vector", "tt", tv)
            fs = fstage[(oc * 2 + th) % 2]
            flane = "fs0" if (oc * 2 + th) % 2 == 0 else "fs1"
            if len(aw_hist) >= 2:
                p.wait("vector", aw_hist[-2][1], aw_hist[-2][0])
            vw = p.op("vector", lambda v, fs=fs: v.tensor_copy(fs[:], pw[:]), sem="vv")
            vw_hist.append(vw)
            p.wait("sync", "vv", vw)
            aw_hist.append((p.dma("sync", att_rows[oc][:, th * 1024:(th + 1) * 1024], fs[:], flane), flane))

    if debug >= 5:
        p.wait("sync", "fs0", p.cnt["fs0"])
        p.wait("sync", "fs1", p.cnt["fs1"])
        asv = dbg["att"].rearrange("(c o) s -> c o s", o=128)
        astg = sb(es4, "astg", [128, 2048], F32)
        for c in range(NC):
            lda = p.dma("sync", astg[:], att_rows[c], "ldw")
            p.wait("sync", "ldw", lda)
            p.dma("sync", asv[c], astg[:], "st")
    es4.close()
    if upto < 5:
        return _finish(nc, p, [es5, esP])

    fchunk = [sb(es5, f"fchunk{i}", [128, NC, 128], F32) for i in range(2)]
    achunk = [sb(es5, f"achunk{i}", [128, NC, 128], F32) for i in range(2)]
    ostage = [sb(es5, f"ostage{i}", [128, D], F32) for i in range(2)]
    xtile = [sb(es5, f"xtile5{i}", [128, D], F32) for i in range(2)]
    po = [es5.enter_context(nc.psum_tensor(f"po{i}", [128, 1024], F32)) for i in range(2)]
    out_tiles = out_hbm.rearrange("(n p) d -> n p d", p=128)
    ffn_cv = ffn_scr.rearrange("(c p) s -> p c s", p=128)
    att_cv = att_scr.rearrange("(c p) s -> p c s", p=128)
    p.wait("sync", "fs0", p.cnt["fs0"])
    p.wait("sync", "fs1", p.cnt["fs1"])
    p.wait("scalar", "fs0", p.cnt["fs0"])
    p.wait("scalar", "fs1", p.cnt["fs1"])
    p.wait("tensor", "vv", p.cnt["vv"])
    odone = {}
    tdone = {}
    stly = []
    for tt_ in range(NT):
        bi = tt_ % 2
        if tt_ >= 2:
            p.wait("sync", "tt", tdone[tt_ - 2])
            p.wait("scalar", "vv", odone[tt_ - 2])
        clane = "lw0" if bi == 0 else "lw1"
        xlane = "lx0" if bi == 0 else "lx1"
        lf = p.dma("sync", fchunk[bi][:], ffn_cv[:, :, tt_ * 128:(tt_ + 1) * 128], clane)
        la = p.dma("sync", achunk[bi][:], att_cv[:, :, tt_ * 128:(tt_ + 1) * 128], clane)
        lx = p.dma("scalar", xtile[bi][:], x_tiles[tt_], xlane)
        p.wait("tensor", clane, la)
        if tt_ >= 2:
            p.wait("tensor", "vv", odone[tt_ - 2])
        pot = po[bi]
        for c in range(NC):
            p.op("tensor", lambda t, c=c, bi=bi, pot=pot, st=(c % 4 == 0): t.matmul(
                pot[:, c * 128:(c + 1) * 128], fchunk[bi][:, c, :], idmf[:],
                start=st, stop=False, is_transpose=True))
        for c in range(NC):
            p.op("tensor", lambda t, c=c, bi=bi, pot=pot, sp=(c % 4 == 3): t.matmul(
                pot[:, c * 128:(c + 1) * 128], achunk[bi][:, c, :], idmf[:],
                start=False, stop=sp, is_transpose=True),
                sem="tt" if c == NC - 1 else None)
        tdone[tt_] = p.cnt["tt"]
        p.wait("vector", "tt", tdone[tt_])
        p.wait("vector", xlane, lx)
        if tt_ >= 2:
            p.wait("vector", "so0" if tt_ % 2 == 0 else "so1", stly[tt_ - 2])
        odone[tt_] = p.op("vector", lambda v, bi=bi, pot=pot: v.tensor_tensor(
            ostage[bi][:], pot[:], xtile[bi][:], ALU.add), sem="vv")
        p.wait("sync", "vv", odone[tt_])
        stly.append(p.dma("sync", out_tiles[tt_], ostage[bi][:], "so0" if bi == 0 else "so1"))

    for s_ in ["st", "so0", "so1", "fs0", "fs1", "dd", "do", "dh0", "dh1", "gst"]:
        p.wait("sync", s_, p.cnt[s_])

    return _finish(nc, p, [es5, esP])


def _emit_guard(nc, p, stacks):
    return _finish(nc, p, stacks)


def _finish(nc, p, stacks):
    for s_, v_ in sorted(p.cnt.items()):
        if v_ > 0:
            p.wait("sync", s_, v_)
    sem_es = ExitStack()
    try:
        for name in SEMS:
            p.handles[name] = sem_es.enter_context(nc.semaphore(name))
        nc.all_engine_barrier()
        for name in SEMS:
            nc.gpsimd.sem_clear(p.handles[name])
        nc.all_engine_barrier()
        with nc.Block() as block:
            @block.sync
            def _(e):
                for f in p.ops["sync"]:
                    f(e)

            @block.gpsimd
            def _(e):
                for f in p.ops["gpsimd"]:
                    f(e)

            @block.vector
            def _(e):
                for f in p.ops["vector"]:
                    f(e)

            @block.scalar
            def _(e):
                for f in p.ops["scalar"]:
                    f(e)

            @block.tensor
            def _(e):
                for f in p.ops["tensor"]:
                    f(e)
    finally:
        sem_es.close()
        for st_ in stacks:
            try:
                st_.close()
            except Exception:
                pass

    return nc


def _host_inputs(inputs):
    x = np.ascontiguousarray(np.asarray(inputs["L0_attn_x"], dtype=np.float32))
    table = np.asarray(inputs["L0_attn_rel_bias_table"], dtype=np.float32)
    shared = {
        "wqT": np.ascontiguousarray(np.asarray(inputs["L0_attn_w_q"], np.float32).T),
        "wkT": np.ascontiguousarray(np.asarray(inputs["L0_attn_w_k"], np.float32).T),
        "wvT": np.ascontiguousarray(np.asarray(inputs["L0_attn_w_v"], np.float32).T),
        "woT": np.ascontiguousarray(np.asarray(inputs["L0_attn_w_o"], np.float32).T),
        "wgT": np.ascontiguousarray(np.asarray(inputs["L0_ffn_w_gate"], np.float32).T),
        "wuT": np.ascontiguousarray(np.asarray(inputs["L0_ffn_w_up"], np.float32).T),
        "wdT": np.ascontiguousarray(np.asarray(inputs["L0_ffn_w_down"], np.float32).T),
        "tabT": np.ascontiguousarray(table.T),
        "idm": np.eye(128, dtype=np.float32),
        "bfar": np.ascontiguousarray(
            np.concatenate([table[15, :], table[31, :]])[None, :]),
    }
    d = np.arange(-(S - 1), S)
    bv = table[_rel_bucket(d), :]
    j = np.arange(BW)
    pp = np.arange(128)
    idx = J0 + j[None, :] - pp[:, None]
    shared["bv2t"] = np.ascontiguousarray(bv[idx, :].transpose(2, 0, 1))
    in_maps = []
    for b in range(8):
        m = dict(shared)
        m["x"] = np.ascontiguousarray(x[b])
        in_maps.append(m)
    return in_maps


def kernel(**inputs):
    in_maps = _host_inputs(inputs)
    nc = build_program(debug=0)
    res = run_bass_kernel_spmd(nc, in_maps, core_ids=list(range(8)))
    return np.stack([r["out"] for r in res.results], axis=0).astype(np.float32)
